# version 01
# baseline (speedup 1.0000x reference)
import numpy as np
import ml_dtypes
import concourse.bass as bass
import concourse.mybir as mybir
from concourse.alu_op_type import AluOpType as OP
from concourse.bass_utils import run_bass_kernel_spmd

F32 = mybir.dt.float32
BF16 = mybir.dt.bfloat16
AF = mybir.ActivationFunctionType
BF = ml_dtypes.bfloat16

D = 4096
T = 1024
NC = 32
TH = 512
HD = 128
SCALE = float(np.float32(1.0) / np.float32(np.sqrt(np.float32(HD))))
EXPC = float(np.exp(np.float32(0.001)))
BSHIFT = -24.0
LNEPS = 1e-5

N_SUB = 4
DEBUG_TAP = None

SEMS = ("sW", "sD", "sD2", "sT", "sV", "sS", "sC", "sA", "sB",
        "sM1", "sM2", "sM3", "sM4", "sE", "sF")



def _core_positions(c):
    if c % 2 == 0:
        return np.concatenate([np.arange(0, 512), np.arange(1536, 2048)])
    return np.arange(512, 1536)


def _pair_perm(n_heads):
    perm = []
    for p in range(n_heads // 2):
        for h in (2 * p, 2 * p + 1):
            perm.extend((h * 128 + np.arange(0, 128, 2)).tolist())
        for h in (2 * p, 2 * p + 1):
            perm.extend((h * 128 + np.arange(1, 128, 2)).tolist())
    return np.array(perm)


def _cos_sin(pos):
    inv = (1.0 / 10000.0 ** (np.arange(0, HD, 2, dtype=np.float32) / HD)).astype(np.float32)
    ang = (pos.astype(np.float32)[:, None] * inv[None, :]).astype(np.float32)
    cos = np.cos(ang).astype(np.float32).T
    sin = np.sin(ang).astype(np.float32).T
    return (np.ascontiguousarray(np.concatenate([cos, cos], 0)),
            np.ascontiguousarray(np.concatenate([sin, sin], 0)))


def _causal_masks(c):
    qpos = [0, 1536] if c % 2 == 0 else [512, 1024]
    tiles = np.zeros((16, 128, 512), np.float32)
    kk = np.arange(128)[:, None]
    qq = np.arange(512)[None, :]
    i = 0
    for s in (0, 1):
        for ch in (range(0, 8) if s == 0 else range(8, 16)):
            tiles[i] = ((128 * ch + kk) <= (qpos[s] + qq)).astype(np.float32)
            i += 1
    return np.ascontiguousarray(tiles.reshape(16 * 128, 512).astype(BF))



class R:
    def __init__(self):
        self.ops = {e: [] for e in ("sync", "gpsimd", "scalar", "vector", "tensor")}
        self.n = {s: 0 for s in SEMS}

    def wait(self, eng, sem, val):
        if val is None or val <= 0:
            return
        def f(E, S, sem=sem, val=val):
            E.wait_ge(S[sem], val)
        self.ops[eng].append(f)

    def do(self, eng, build, inc=None, k=1):
        tgt = None
        if inc is not None:
            self.n[inc] += k
            tgt = self.n[inc]
        def f(E, S, build=build, inc=inc, k=k):
            ins = build(E, S)
            if inc is not None:
                ins.then_inc(S[inc], k)
        self.ops[eng].append(f)
        return tgt



def build_program(n_sub=4, debug_tap=None):
    nc = bass.Bass("TRN2", target_bir_lowering=False)
    dt = lambda n, s, ty, k="ExternalInput": nc.dram_tensor(n, s, ty, kind=k).ap()

    x_in = dt("x_in", [D, T], F32)
    wqT = dt("wqT", [D, D], F32)
    wkT = dt("wkT", [D, D], F32)
    wvT = dt("wvT", [D, D], F32)
    woT = dt("woT", [D, D], F32)
    bq_t = dt("bq_t", [32, 128], F32)
    bk_t = dt("bk_t", [32, 128], F32)
    bv_d = dt("bv_d", [D], F32)
    bo_d = dt("bo_d", [D], F32)
    wQb = dt("wQb", [D, 4096], F32)
    wKb = dt("wKb", [D, 1024], F32)
    wVb = dt("wVb", [D, 1024], F32)
    woB = dt("woB", [D, D], F32)
    wbaseT = dt("wbaseT", [D, D], F32)
    bbase_d = dt("bbase_d", [D], F32)
    wrouT = dt("wrouT", [D, 8], F32)
    brou = dt("brou", [8, 1], F32)
    a_all = dt("a_all", [D, 128], F32)
    b_all = dt("b_all", [128, D], F32)
    cos_d = dt("cos_d", [128, T], F32)
    sin_d = dt("sin_d", [128, T], F32)
    cmask_d = dt("cmask_d", [16 * 128, 512], BF16)
    maskil_d = dt("maskil_d", [128, 128], BF16)
    id32_d = dt("id32_d", [128, 128], F32)
    ones_d = dt("ones_d", [128, 512], BF16)
    onesf_d = dt("onesf_d", [1, 128], F32)
    prep_d = dt("prep_d", [8, 128], BF16)

    out = dt("out", [D, T], F32, "ExternalOutput")
    dbg = dt("dbg", [D, T], F32, "ExternalOutput") if debug_tap is not None else None

    QA_d = dt("QA_d", [D, T], BF16, "Internal")
    KA_d = dt("KA_d", [D, T], BF16, "Internal")
    VA_d = dt("VA_d", [T * 32, 128], BF16, "Internal")
    QB_d = dt("QB_d", [D, T], BF16, "Internal")
    agin_t = nc.dram_tensor("agin", [2048, 1024], BF16, kind="Internal")
    agout_t = nc.dram_tensor("agout", [4096, 1024], BF16, kind="Internal")
    agin = agin_t.ap()
    agout = agout_t.ap()

    r = R()

    from contextlib import ExitStack
    with ExitStack() as _es:
        en = _es.enter_context
        X = en(nc.sbuf_tensor("X", [128, NC * T], BF16))
        B2 = en(nc.sbuf_tensor("B2", [128, NC * T], BF16))
        SCR = en(nc.sbuf_tensor("SCR", [128, 16384], BF16))
        WP = en(nc.sbuf_tensor("WP", [128, 8192], BF16))
        SP = en(nc.sbuf_tensor("SP", [128, 1024], BF16))
        AB = en(nc.sbuf_tensor("AB", [128, 2048], F32))
        RB = en(nc.sbuf_tensor("RB", [128, 1024], F32))
        ST = en(nc.sbuf_tensor("ST", [128, 1024], F32))
        SMX = en(nc.sbuf_tensor("SMX", [128, 384], F32))
        CB = en(nc.sbuf_tensor("CB", [128, 128], F32))
        COS2 = en(nc.sbuf_tensor("COS2", [128, T], F32))
        SIN2 = en(nc.sbuf_tensor("SIN2", [128, T], F32))
        MIL = en(nc.sbuf_tensor("MIL", [128, 128], BF16))
        ID32 = en(nc.sbuf_tensor("ID32", [128, 128], F32))
        ONES = en(nc.sbuf_tensor("ONES", [128, 512], BF16))
        ONESF = en(nc.sbuf_tensor("ONESF", [1, 128], F32))
        PREP = en(nc.sbuf_tensor("PREP", [8, 128], BF16))
        p0 = en(nc.psum_tensor("p0", [128, 512], F32))
        p1 = en(nc.psum_tensor("p1", [128, 512], F32))
        p2 = en(nc.psum_tensor("p2", [128, 512], F32))
        p3 = en(nc.psum_tensor("p3", [128, 512], F32))
        p4 = en(nc.psum_tensor("p4", [128, 512], F32))
        p5 = en(nc.psum_tensor("p5", [128, 512], F32))
        p6 = en(nc.psum_tensor("p6", [128, 512], F32))
        p7 = en(nc.psum_tensor("p7", [128, 512], F32))
        hW = en(nc.semaphore("sW"))
        hD = en(nc.semaphore("sD"))
        hD2 = en(nc.semaphore("sD2"))
        hT = en(nc.semaphore("sT"))
        hV = en(nc.semaphore("sV"))
        hS = en(nc.semaphore("sS"))
        hC = en(nc.semaphore("sC"))
        hA = en(nc.semaphore("sA"))
        hB = en(nc.semaphore("sB"))
        hM1 = en(nc.semaphore("sM1"))
        hM2 = en(nc.semaphore("sM2"))
        hM3 = en(nc.semaphore("sM3"))
        hM4 = en(nc.semaphore("sM4"))
        hE = en(nc.semaphore("sE"))
        hF = en(nc.semaphore("sF"))
        block = en(nc.Block())
        S = {"sW": hW, "sD": hD, "sD2": hD2, "sT": hT, "sV": hV, "sS": hS, "sC": hC,
             "sA": hA, "sB": hB, "sM1": hM1, "sM2": hM2, "sM3": hM3, "sM4": hM4,
             "sE": hE, "sF": hF}
        P = [p0, p1, p2, p3, p4, p5, p6, p7]
        STbf = ST[:].bitcast(BF16)
        ABbf = AB[:].bitcast(BF16)

        bank_free = [None] * 8

        sp_state = {"last": ("sD", 0)}

        def bank_wait(i, eng="tensor"):
            if bank_free[i] is not None:
                sem, val = bank_free[i]
                r.wait(eng, sem, val)
                bank_free[i] = None

        def dma(eng, sem, outap, inap):
            return r.do(eng, lambda E, S, o=outap, i=inap: E.dma_start(out=o, in_=i),
                        inc=sem, k=16)

        dma("sync", "sD", COS2[:], cos_d)
        dma("sync", "sD", SIN2[:], sin_d)
        dma("sync", "sD", MIL[:], maskil_d)
        dma("sync", "sD", ID32[:], id32_d)
        dma("sync", "sD", ONES[:], ones_d)
        dma("sync", "sD", ONESF[:], onesf_d)
        dma("sync", "sD", PREP[:], prep_d)
        dma("sync", "sD", CB[0:8, 64:65], brou)
        dma("sync", "sD", RB[0:32, 0:128], bq_t)
        v_setup = dma("sync", "sD", RB[0:32, 128:256], bk_t)

        dma("gpsimd", "sW", X[:].rearrange("d (c t) -> d c t", t=T),
            x_in.rearrange("(c d) t -> d c t", d=128))
        w_xload = r.n["sW"]

        r.wait("tensor", "sD", v_setup)
        r.do("tensor", lambda E, S: E.matmul(p7[:, 0:32], RB[0:32, 0:128],
                                             ID32[0:32, 0:32], start=True, stop=True))
        t_bias = r.do("tensor", lambda E, S: E.matmul(p7[:, 32:64], RB[0:32, 128:256],
                                                      ID32[0:32, 0:32], start=True, stop=True),
                      inc="sT")
        r.wait("vector", "sT", t_bias)
        v_bias = r.do("vector", lambda E, S: E.tensor_copy(CB[:, 0:64], p7[:, 0:64]),
                      inc="sV")
        bank_free[7] = ("sV", v_bias)

        def emit_ln(xr_sem, xr_val):
            XSQ = [SCR[:, 0:1024], SCR[:, 1024:2048]]
            r.wait("scalar", xr_sem, xr_val)
            r.wait("tensor", xr_sem, xr_val)
            r.do("vector", lambda E, S: E.drain())
            for i in range(4):
                bank_wait(i)
            sq_done = [None] * NC
            mm_done = [None] * NC
            for j in range(NC):
                if j >= 2:
                    r.wait("scalar", "sT", mm_done[j - 2])
                sq_done[j] = r.do("scalar", lambda E, S, j=j: E.activation(
                    out=XSQ[j % 2], in_=X[:, j * T:(j + 1) * T], func=AF.Square), inc="sS")
                r.wait("tensor", "sS", sq_done[j])
                st, sp = (j == 0), (j == NC - 1)
                r.do("tensor", lambda E, S, j=j, st=st, sp=sp: E.matmul(
                    p0[0:1, :], ONES[:, 0:1], X[:, j * T:j * T + TH], start=st, stop=sp))
                r.do("tensor", lambda E, S, j=j, st=st, sp=sp: E.matmul(
                    p1[0:1, :], ONES[:, 0:1], X[:, j * T + TH:(j + 1) * T], start=st, stop=sp))
                r.do("tensor", lambda E, S, j=j, st=st, sp=sp: E.matmul(
                    p2[0:1, :], ONES[:, 0:1], XSQ[j % 2][:, 0:TH], start=st, stop=sp))
                mm_done[j] = r.do("tensor", lambda E, S, j=j, st=st, sp=sp: E.matmul(
                    p3[0:1, :], ONES[:, 0:1], XSQ[j % 2][:, TH:T], start=st, stop=sp), inc="sT")
            r.wait("scalar", "sT", mm_done[NC - 1])
            inv = 1.0 / D
            prev_t1 = prev_v2 = None
            for h in range(2):
                psx, psxx = (p0, p2) if h == 0 else (p1, p3)
                if h == 1:
                    r.wait("scalar", "sT", prev_t1)
                    r.wait("scalar", "sV", prev_v2)
                SCRf = SCR[:].bitcast(F32)
                sm = SCRf[0:1, 1024:1536]
                sm2 = SCRf[0:1, 1536:2048]
                sx3 = SCRf[0:1, 2048:2560]
                r.do("scalar", lambda E, S, psx=psx, sm=sm: E.activation(
                    out=sm, in_=psx[0:1, :], func=AF.Copy, scale=inv))
                r.do("scalar", lambda E, S, psx=psx, sm2=sm2: E.activation(
                    out=sm2, in_=psx[0:1, :], func=AF.Square, scale=inv))
                s1 = r.do("scalar", lambda E, S, psxx=psxx, sx3=sx3: E.activation(
                    out=sx3, in_=psxx[0:1, :], func=AF.Copy, scale=inv), inc="sS")
                r.wait("vector", "sS", s1)
                v1 = r.do("vector", lambda E, S, sm2=sm2, sx3=sx3: E.scalar_tensor_tensor(
                    out=sm2, in0=sx3, scalar=LNEPS, in1=sm2,
                    op0=OP.add, op1=OP.subtract), inc="sV")
                r.wait("scalar", "sV", v1)
                r.do("scalar", lambda E, S, sm2=sm2, sx3=sx3: E.activation(
                    out=sx3, in_=sm2, func=AF.Ln))
                r.do("scalar", lambda E, S: E.drain())
                s2 = r.do("scalar", lambda E, S, sx3=sx3: E.activation(
                    out=ST[0:1, 0:512], in_=sx3, func=AF.Exp, scale=-0.5),
                    inc="sS")
                r.wait("vector", "sS", s2)
                v2 = r.do("vector", lambda E, S, sm=sm: E.scalar_tensor_tensor(
                    out=ST[0:1, 512:1024], in0=sm, scalar=-1.0,
                    in1=ST[0:1, 0:512], op0=OP.mult, op1=OP.mult), inc="sV")
                bank_wait(4)
                bank_wait(5)
                r.wait("tensor", "sV", v2)
                r.do("tensor", lambda E, S: E.matmul(p4[:], ONESF[:], ST[0:1, 0:512],
                                                     start=True, stop=True))
                t1 = r.do("tensor", lambda E, S: E.matmul(p5[:], ONESF[:],
                                                          ST[0:1, 512:1024],
                                                          start=True, stop=True), inc="sT")
                r.wait("vector", "sT", t1)
                r.do("vector", lambda E, S, h=h: E.tensor_copy(AB[:, h * TH:(h + 1) * TH],
                                                               p4[:]))
                v3 = r.do("vector", lambda E, S, h=h: E.tensor_copy(
                    AB[:, 1024 + h * TH:1024 + (h + 1) * TH], p5[:]), inc="sV")
                bank_free[4] = ("sV", v3)
                bank_free[5] = ("sV", v3)
                prev_t1, prev_v2 = t1, v2
            for i, v in ((0, None), (1, None), (2, None), (3, None)):
                bank_free[i] = ("sS", r.n["sS"])
            r.do("vector", lambda E, S: E.drain())
            last = None
            for j in range(NC):
                for h in range(2):
                    c0 = j * T + h * TH
                    r.do("vector", lambda E, S, c0=c0, h=h: E.tensor_tensor(
                        out=RB[:, h * TH:(h + 1) * TH], in0=X[:, c0:c0 + TH],
                        in1=AB[:, h * TH:(h + 1) * TH], op=OP.mult))
                r.do("vector", lambda E, S: E.drain())
                for h in range(2):
                    c0 = j * T + h * TH
                    last = r.do("vector", lambda E, S, c0=c0, h=h: E.tensor_tensor(
                        out=B2[:, c0:c0 + TH], in0=RB[:, h * TH:(h + 1) * TH],
                        in1=AB[:, 1024 + h * TH:1024 + (h + 1) * TH], op=OP.add), inc="sV")
                r.do("vector", lambda E, S: E.drain())
            return last

        def fm_proj(Wd, n_o, rhs_ready, sink, bias_row=None, extra=None,
                    banks=(0, 1, 2, 3)):
            panel_consumed = [None] * n_o
            for o in range(n_o):
                wp = WP[:, (o % 2) * 4096:(o % 2) * 4096 + 4096]
                psem = "sA" if o % 2 == 0 else "sB"
                if o >= 2:
                    r.wait("gpsimd", "sT", panel_consumed[o - 2])
                else:
                    r.wait("gpsimd", "sT", r.n["sT"])
                wa = dma("gpsimd", psem, wp.rearrange("d (k o) -> d k o", o=128),
                         Wd.rearrange("(k d) o -> d k o", d=128)[:, :, o * 128:(o + 1) * 128])
                r.wait("tensor", psem, wa)
                if o == 0:
                    r.wait("tensor", rhs_ready[0], rhs_ready[1])
                for tc in range(2):
                    bi = banks[(2 * o + tc) % len(banks)]
                    bank_wait(bi)
                    for k in range(NC):
                        st = (k == 0)
                        last = (k == NC - 1) and bias_row is None and extra is None
                        tv0 = r.do("tensor",
                                   lambda E, S, bi=bi, k=k, tc=tc, wp=wp, st=st, sp=last:
                                   E.matmul(P[bi][:], wp[:, k * 128:(k + 1) * 128],
                                            B2[:, k * T + tc * TH:k * T + tc * TH + TH],
                                            start=st, stop=sp),
                                   inc="sT" if last else None)
                        if last:
                            tv = tv0
                    if bias_row is not None:
                        last = extra is None
                        tv0 = r.do("tensor",
                                   lambda E, S, bi=bi, o=o, br=bias_row, sp=last:
                                   E.matmul(P[bi][:], br[0:1, o * 128:(o + 1) * 128],
                                            ONES[0:1, 0:TH], start=False, stop=sp),
                                   inc="sT" if last else None)
                        if last:
                            tv = tv0
                    if extra is not None:
                        tv = extra(o, tc, bi)
                    bank_free[bi] = sink(o, tc, bi, tv)
                panel_consumed[o] = r.n["sT"]

        def sink_spill(dram, bias_col=None):
            spill_done = {}
            def sink(o, tc, bi, tv):
                sb = SP[:, tc * TH:(tc + 1) * TH]
                if o == 0 and tc == 0:
                    r.wait("scalar", sp_state["last"][0], sp_state["last"][1])
                if o >= 1 and (o - 1) in spill_done and tc == 0:
                    r.wait("scalar", "sD", spill_done[o - 1])
                r.wait("scalar", "sT", tv)
                if bias_col is not None:
                    sv = r.do("scalar", lambda E, S, sb=sb, bi=bi, o=o: E.activation(
                        out=sb, in_=P[bi][:], func=AF.Identity,
                        bias=CB[:, bias_col + o:bias_col + o + 1]), inc="sS")
                else:
                    sv = r.do("scalar", lambda E, S, sb=sb, bi=bi: E.activation(
                        out=sb, in_=P[bi][:], func=AF.Copy), inc="sS")
                if tc == 1:
                    r.wait("sync", "sS", sv)
                    spill_done[o] = dma("sync", "sD", dram[o * 128:(o + 1) * 128, :],
                                        SP[:, 0:1024])
                    sp_state["last"] = ("sD", spill_done[o])
                return ("sS", sv)
            sink.spill_done = spill_done
            return sink

        def sink_resid():
            def sink(o, tc, bi, tv):
                r.wait("vector", "sT", tv)
                c0 = o * T + tc * TH
                vv = r.do("vector", lambda E, S, c0=c0, bi=bi: E.tensor_tensor(
                    out=X[:, c0:c0 + TH], in0=P[bi][:], in1=X[:, c0:c0 + TH],
                    op=OP.add), inc="sV")
                return ("sV", vv)
            return sink

        def load_bias_row(dst_row, dd, n, after_sem, after_val, sem="sM1"):
            r.wait("gpsimd", after_sem, after_val)
            return dma("gpsimd", sem, dst_row[0:1, 0:n],
                       dd.rearrange("(a b) -> a b", a=1))

        def tm_proj(Wd, rhs_ready, dst_fn, n_oh=2, bias_bf_row=None):
            r.wait("tensor", rhs_ready[0], rhs_ready[1])
            tiles = [SCR[:, 2048:2560], SCR[:, 2560:3072]]
            half_done = {}
            prev_used = None
            for oh in range(n_oh):
                tile_used = [None] * NC
                for k in range(NC):
                    tsem = "sA" if k % 2 == 0 else "sB"
                    if k >= 2:
                        r.wait("gpsimd", "sT", tile_used[k - 2])
                    elif prev_used is not None:
                        r.wait("gpsimd", "sT", prev_used[NC - 2 + k])
                    else:
                        r.wait("gpsimd", "sT", r.n["sT"])
                    wa = dma("gpsimd", tsem, tiles[k % 2],
                             Wd.rearrange("(k d) o -> d k o", d=128)[:, k,
                                          oh * 512:(oh + 1) * 512])
                    r.wait("tensor", tsem, wa)
                    for tch in range(8):
                        if k == 0:
                            bank_wait(tch)
                        st = (k == 0)
                        sp = (k == NC - 1) and bias_bf_row is None
                        tu = r.do("tensor",
                                  lambda E, S, tch=tch, k=k, st=st, sp=sp, tl=tiles[k % 2]:
                                  E.matmul(P[tch][:],
                                           B2[:, k * T + tch * 128:k * T + (tch + 1) * 128],
                                           tl, start=st, stop=sp),
                                  inc="sT" if tch == 7 else None)
                    tile_used[k] = tu
                prev_used = tile_used
                for tch in range(8):
                    if bias_bf_row is not None:
                        tv = r.do("tensor", lambda E, S, tch=tch, oh=oh, br=bias_bf_row:
                                  E.matmul(P[tch][:], ONES[0:1, 0:128],
                                           br[0:1, oh * 512:(oh + 1) * 512],
                                           start=False, stop=True), inc="sT")
                    else:
                        tv = tile_used[NC - 1]
                    hsem = "sE" if oh % 2 == 0 else "sF"
                    if (oh % 2) in half_done:
                        r.wait("scalar", hsem, half_done[oh % 2])
                    elif oh < 2 and tch == 0:
                        r.wait("scalar", sp_state["last"][0], sp_state["last"][1])
                    r.wait("scalar", "sT", tv)
                    sv = r.do("scalar", lambda E, S, tch=tch, oh=oh: E.activation(
                        out=SP[:, (oh % 2) * TH:(oh % 2 + 1) * TH],
                        in_=P[tch][:], func=AF.Copy), inc="sS")
                    bank_free[tch] = ("sS", sv)
                    r.wait("sync", "sS", sv)
                    oap, iap = dst_fn(tch, oh, SP[:, (oh % 2) * TH:(oh % 2 + 1) * TH])
                    half_done[oh % 2] = dma("sync", hsem, oap, iap)
                    sp_state["last"] = (hsem, half_done[oh % 2])
            return (r.n["sE"], r.n["sF"])

        tap_points = {}

        def sub1():
            ln1 = emit_ln("sW", w_xload)
            skq = sink_spill(QA_d, bias_col=0)
            fm_proj(wqT, NC, ("sV", ln1), skq)
            skk = sink_spill(KA_d, bias_col=32)
            fm_proj(wkT, NC, ("sV", ln1), skk)
            bw = load_bias_row(ABbf, bv_d, D, "sV", ln1)
            r.wait("tensor", "sM1", bw)
            va_view = VA_d.rearrange("(tt g tp) d -> tt tp g d", g=32, tp=128)

            def va_dst(tch, oh, src):
                return (va_view[tch][:, 4 * oh:4 * oh + 4, :],
                        src.rearrange("p (g4 d) -> p g4 d", d=128))
            vE, vF = tm_proj(wvT, ("sV", ln1), va_dst, n_oh=8, bias_bf_row=ABbf)
            bo_w = load_bias_row(ABbf, bo_d, D, "sT", r.n["sT"])

            spills = max(max(skq.spill_done.values()), max(skk.spill_done.values()))

            QC, KC, VR, EN = (SCR[:, 0:4096], SCR[:, 4096:8192],
                              SCR[:, 8192:12288], SCR[:, 12288:16384])
            ao_last = None
            for tt in range(8):
                t0 = tt * 128
                r.wait("sync", "sT", r.n["sT"])
                r.wait("sync", "sD", spills)
                r.wait("sync", "sE", vE)
                r.wait("sync", "sF", vF)
                dma("sync", "sD", QC.rearrange("d (h t) -> d h t", t=128),
                    QA_d.rearrange("(h d) t -> d h t", d=128)[:, :, t0:t0 + 128])
                dma("sync", "sD", KC.rearrange("d (h t) -> d h t", t=128),
                    KA_d.rearrange("(h d) t -> d h t", d=128)[:, :, t0:t0 + 128])
                ld = dma("sync", "sD",
                         VR.rearrange("p (gp d) -> p gp d", d=128),
                         VA_d.rearrange("(tt g j gp) d -> tt (g j) gp d",
                                        tt=8, g=32, j=4)[tt])
                r.wait("tensor", "sD", ld)
                for gq in range(8):
                    sbank = P[gq % 2]
                    pvbank = P[5 + (gq % 2)]
                    bank_wait(gq % 2)
                    for gg in range(4):
                        g = gq * 4 + gg
                        t_s = r.do("tensor", lambda E, S, g=g, gg=gg, sbank=sbank: E.matmul(
                            sbank[:, gg * 128:(gg + 1) * 128], KC[:, g:4096:32],
                            QC[:, g:4096:32], start=True, stop=True),
                            inc="sT" if gg == 3 else None)
                    r.wait("scalar", "sT", t_s)
                    s_e = r.do("scalar", lambda E, S, gq=gq, sbank=sbank: E.activation(
                        out=EN[:, gq * 512:(gq + 1) * 512], in_=sbank[:], func=AF.Exp,
                        scale=SCALE), inc="sS")
                    r.wait("vector", "sS", s_e)
                    r.do("vector", lambda E, S, gq=gq, sbank=sbank: E.tensor_tensor(
                        out=sbank[:].rearrange("p (a b) -> p a b", b=128),
                        in0=MIL[:].unsqueeze(1).broadcast_to([128, 4, 128]),
                        in1=EN[:, gq * 512:(gq + 1) * 512].rearrange("p (a b) -> p a b",
                                                                     b=128),
                        op=OP.mult))
                    r.do("vector", lambda E, S: E.drain())
                    v_u = r.do("vector", lambda E, S, gq=gq, sbank=sbank:
                               E.scalar_tensor_tensor(
                                   out=EN[:, gq * 512:(gq + 1) * 512], in0=sbank[:],
                                   scalar=EXPC, in1=EN[:, gq * 512:(gq + 1) * 512],
                                   op0=OP.is_gt, op1=OP.mult), inc="sV")
                    bank_free[gq % 2] = ("sV", v_u)
                    bank_wait(4)
                    bank_wait(5 + (gq % 2))
                    r.wait("tensor", "sV", v_u)
                    for gg in range(4):
                        g = gq * 4 + gg
                        r.do("tensor", lambda E, S, g=g, gg=gg: E.matmul(
                            p4[0:1, gg * 128:(gg + 1) * 128], ONES[:, 0:1],
                            EN[:, g * 128:(g + 1) * 128], start=True, stop=True))
                        t_d = r.do("tensor", lambda E, S, g=g, gg=gg, pvbank=pvbank: E.matmul(
                            pvbank[:, gg * 128:(gg + 1) * 128], VR[:, g * 128:(g + 1) * 128],
                            EN[:, g * 128:(g + 1) * 128], start=True, stop=True),
                            inc="sT" if gg == 3 else None)
                    r.wait("scalar", "sT", t_d)
                    s_l = r.do("scalar", lambda E, S: E.activation(
                        out=ST[0:1, 0:512], in_=p4[0:1, :], func=AF.Ln), inc="sS")
                    bank_free[4] = ("sS", s_l)
                    bank_wait(7)
                    r.wait("tensor", "sS", s_l)
                    t_b = r.do("tensor", lambda E, S: E.matmul(
                        p7[:], ONESF[:], ST[0:1, 0:512], start=True, stop=True), inc="sT")
                    r.wait("scalar", "sT", t_b)
                    s_r = r.do("scalar", lambda E, S, gq=gq: E.activation(
                        out=RB[:, (gq % 2) * 512:(gq % 2) * 512 + 512], in_=p7[:],
                        func=AF.Exp, scale=-1.0), inc="sS")
                    bank_free[7] = ("sS", s_r)
                    r.wait("vector", "sS", s_r)
                    dst = B2[:].rearrange("p (h jj aa) -> p aa h jj", h=32, jj=32, aa=32)[
                        :, 4 * gq:4 * gq + 4, :, 4 * tt:4 * tt + 4]
                    src = pvbank[:].rearrange("p (a h j) -> p a h j", a=4, h=32)
                    rrv = RB[:, (gq % 2) * 512:(gq % 2) * 512 + 512].rearrange(
                        "p (a h j) -> p a h j", a=4, h=32)
                    ao_last = r.do("vector", lambda E, S, dst=dst, src=src, rrv=rrv:
                                   E.tensor_tensor(out=dst, in0=src, in1=rrv, op=OP.mult),
                                   inc="sV")
                    bank_free[5 + (gq % 2)] = ("sV", ao_last)
            r.wait("tensor", "sM1", bo_w)
            fm_proj(woT, NC, ("sV", ao_last), sink_resid(), bias_row=ABbf)
            return r.n["sV"]

        def sub_ffn(prev_v):
            ln = emit_ln("sV", prev_v)
            bw = load_bias_row(ABbf, bbase_d, D, "sV", ln)
            ROUT = SCR[:, 0:256]
            rw = dma("gpsimd", "sM2", ROUT.rearrange("d (k e) -> d k e", e=8),
                     wrouT.rearrange("(k d) e -> d k e", d=128))
            r.wait("tensor", "sM2", rw)
            r.wait("tensor", "sV", ln)
            bank_wait(4)
            bank_wait(5)
            for tc in range(2):
                pb = p4 if tc == 0 else p5
                for k in range(NC):
                    st, sp = (k == 0), (k == NC - 1)
                    t_r = r.do("tensor",
                               lambda E, S, pb=pb, k=k, tc=tc, st=st, sp=sp: E.matmul(
                                   pb[0:8, :], ROUT[:, k * 8:(k + 1) * 8],
                                   B2[:, k * T + tc * TH:k * T + tc * TH + TH],
                                   start=st, stop=sp),
                               inc="sT" if (tc == 1 and k == NC - 1) else None)
            r.wait("scalar", "sT", t_r)
            for tc in range(2):
                pb = p4 if tc == 0 else p5
                s_lg = r.do("scalar", lambda E, S, pb=pb, tc=tc: E.activation(
                    out=RB[0:8, tc * TH:(tc + 1) * TH], in_=pb[0:8, :], func=AF.Identity,
                    bias=CB[0:8, 64:65]), inc="sS" if tc == 1 else None)
            bank_free[4] = ("sS", s_lg)
            bank_free[5] = ("sS", s_lg)
            bank_wait(6)
            r.wait("tensor", "sS", s_lg)
            for c in range(8):
                t_t = r.do("tensor", lambda E, S, c=c: E.matmul(
                    p6[:, c * 8:(c + 1) * 8], RB[0:8, c * 128:(c + 1) * 128],
                    ID32[0:8, 0:8], start=True, stop=True),
                    inc="sT" if c == 7 else None)
            r.wait("vector", "sT", t_t)
            v_lt = r.do("vector", lambda E, S: E.tensor_copy(SMX[:, 0:64], p6[:, 0:64]),
                        inc="sV")
            bank_free[6] = ("sV", v_lt)
            r.do("vector", lambda E, S: E.drain())

            LT = lambda e: SMX[:, e * 8:e * 8 + 8] if False else None
            cols = lambda b, e: SMX[:, b + e:b + 64:8]
            def vd(build):
                v = r.do("vector", build, inc="sV")
                r.do("vector", lambda E, S: E.drain())
                return v
            vd(lambda E, S: E.tensor_copy(SMX[:, 64:128], SMX[:, 0:64]))
            net = [(0, 1), (2, 3), (4, 5), (6, 7), (0, 2), (1, 3), (4, 6), (5, 7),
                   (1, 2), (5, 6), (0, 4), (3, 7), (1, 5), (2, 6), (1, 4), (3, 6),
                   (2, 4), (3, 5), (3, 4)]
            for (a, b) in net:
                ca, cb = cols(64, a), cols(64, b)
                tmp = SMX[:, 280:288]
                vd(lambda E, S, ca=ca, cb=cb, tmp=tmp: E.tensor_tensor(
                    out=tmp, in0=ca, in1=cb, op=OP.max))
                vd(lambda E, S, ca=ca, cb=cb: E.tensor_tensor(
                    out=cb, in0=ca, in1=cb, op=OP.min))
                vd(lambda E, S, ca=ca, tmp=tmp: E.tensor_copy(ca, tmp))
            vd(lambda E, S: E.tensor_copy(cols(128, 0), cols(64, 0)))
            for e in range(1, 8):
                vd(lambda E, S, e=e: E.tensor_tensor(
                    out=cols(128, e), in0=cols(128, e - 1), in1=cols(64, e), op=OP.add))
            for e in range(8):
                vd(lambda E, S, e=e: E.tensor_scalar(
                    out=SMX[:, 280:288], in0=cols(64, e), scalar1=float(e + 1),
                    scalar2=1.0, op0=OP.mult, op1=OP.add))
                vd(lambda E, S, e=e: E.tensor_tensor(
                    out=cols(192, e), in0=SMX[:, 280:288], in1=cols(128, e), op=OP.is_gt))
            vd(lambda E, S: E.tensor_tensor(
                out=SMX[:, 256:264], in0=cols(192, 0), in1=cols(192, 1), op=OP.add))
            for e in range(2, 8):
                vd(lambda E, S, e=e: E.tensor_tensor(
                    out=SMX[:, 256:264], in0=SMX[:, 256:264], in1=cols(192, e), op=OP.add))
            vd(lambda E, S: E.tensor_tensor(
                out=SMX[:, 264:272], in0=cols(192, 0), in1=cols(64, 0), op=OP.mult))
            for e in range(1, 8):
                vd(lambda E, S, e=e: E.tensor_tensor(
                    out=SMX[:, 280:288], in0=cols(192, e), in1=cols(64, e), op=OP.mult))
                vd(lambda E, S: E.tensor_tensor(
                    out=SMX[:, 264:272], in0=SMX[:, 264:272], in1=SMX[:, 280:288],
                    op=OP.add))
            v_kz = r.n["sV"]
            r.wait("scalar", "sV", v_kz)
            r.do("scalar", lambda E, S: E.activation(out=SMX[:, 280:288],
                                                     in_=SMX[:, 256:264], func=AF.Ln))
            r.do("scalar", lambda E, S: E.drain())
            s_rk = r.do("scalar", lambda E, S: E.activation(
                out=SMX[:, 256:264], in_=SMX[:, 280:288], func=AF.Exp, scale=-1.0),
                inc="sS")
            r.wait("vector", "sS", s_rk)
            vd(lambda E, S: E.scalar_tensor_tensor(
                out=SMX[:, 272:280], in0=SMX[:, 264:272], scalar=-1.0, in1=SMX[:, 256:264],
                op0=OP.add, op1=OP.mult))
            v_w = None
            for e in range(8):
                vd(lambda E, S, e=e: E.tensor_tensor(
                    out=SMX[:, 280:288], in0=cols(0, e), in1=SMX[:, 272:280],
                    op=OP.subtract))
                v_w = vd(lambda E, S, e=e: E.tensor_single_scalar(
                    out=cols(288, e), in_=SMX[:, 280:288], scalar=0.0, op=OP.max))

            WFM = SCR[0:8, 256:1280]
            bank_wait(4)
            bank_wait(5)
            r.wait("tensor", "sV", v_w)
            for c in range(8):
                pb = p4 if c < 4 else p5
                t_w = r.do("tensor", lambda E, S, c=c, pb=pb: E.matmul(
                    pb[0:8, (c % 4) * 128:(c % 4 + 1) * 128],
                    SMX[:, 288 + c * 8:288 + (c + 1) * 8], ID32[:], start=True, stop=True),
                    inc="sT" if c == 7 else None)
            r.wait("scalar", "sT", t_w)
            r.do("scalar", lambda E, S: E.activation(out=WFM[:, 0:512], in_=p4[0:8, :],
                                                     func=AF.Copy))
            s_wf = r.do("scalar", lambda E, S: E.activation(
                out=WFM[:, 512:1024], in_=p5[0:8, :], func=AF.Copy), inc="sS")
            bank_free[4] = ("sS", s_wf)
            bank_free[5] = ("sS", s_wf)
            WREP = SCR[:, 1280:2304]
            bank_wait(6)
            bank_wait(7)
            r.wait("tensor", "sS", s_wf)
            r.do("tensor", lambda E, S: E.matmul(p6[:], PREP[:], WFM[:, 0:512],
                                                 start=True, stop=True))
            t_wr = r.do("tensor", lambda E, S: E.matmul(p7[:], PREP[:], WFM[:, 512:1024],
                                                        start=True, stop=True), inc="sT")
            r.wait("scalar", "sT", t_wr)
            r.do("scalar", lambda E, S: E.activation(out=WREP[:, 0:512], in_=p6[:],
                                                     func=AF.Copy))
            s_wr = r.do("scalar", lambda E, S: E.activation(
                out=WREP[:, 512:1024], in_=p7[:], func=AF.Copy), inc="sS")
            bank_free[6] = ("sS", s_wr)
            bank_free[7] = ("sS", s_wr)

            APAN = SCR[:, 2304:6400]
            aw = dma("gpsimd", "sM3", APAN.rearrange("d (k r) -> d k r", r=128),
                     a_all.rearrange("(k d) r -> d k r", d=128))
            G = SCR[:, 6400:7424]
            bank_wait(2)
            bank_wait(3)
            r.wait("tensor", "sM3", aw)
            for tc in range(2):
                pb = p2 if tc == 0 else p3
                for k in range(NC):
                    st, sp = (k == 0), (k == NC - 1)
                    t_h = r.do("tensor",
                               lambda E, S, pb=pb, k=k, tc=tc, st=st, sp=sp: E.matmul(
                                   pb[:], APAN[:, k * 128:(k + 1) * 128],
                                   B2[:, k * T + tc * TH:k * T + tc * TH + TH],
                                   start=st, stop=sp),
                               inc="sT" if (tc == 1 and k == NC - 1) else None)
            r.wait("vector", "sT", t_h)
            r.wait("vector", "sS", s_wr)
            r.do("vector", lambda E, S: E.tensor_tensor(
                out=G[0:128, 0:512], in0=p2[:], in1=WREP[:, 0:512], op=OP.mult))
            v_g = r.do("vector", lambda E, S: E.tensor_tensor(
                out=G[0:128, 512:1024], in0=p3[:], in1=WREP[:, 512:1024], op=OP.mult),
                inc="sV")
            bank_free[2] = ("sV", v_g)
            bank_free[3] = ("sV", v_g)

            BPAN = SCR[:, 7424:11520]
            bw2 = dma("gpsimd", "sM4", BPAN.rearrange("p (k o) -> p k o", o=128),
                      b_all.rearrange("er (k o) -> er k o", o=128))
            r.wait("tensor", "sM4", bw2)
            r.wait("tensor", "sM1", bw)
            r.wait("tensor", "sV", v_g)

            def extra(o, tc, bi):
                return r.do("tensor", lambda E, S, o=o, tc=tc, bi=bi: E.matmul(
                    P[bi][:], BPAN[:, o * 128:(o + 1) * 128], G[:, tc * TH:(tc + 1) * TH],
                    start=False, stop=True), inc="sT")
            fm_proj(wbaseT, NC, ("sV", v_g), sink_resid(), bias_row=ABbf, extra=extra)
            return r.n["sV"]

        def sub3(prev_v):
            ln = emit_ln("sV", prev_v)
            RPA, RPB = SCR[:, 0:1024], SCR[:, 1024:2048]
            R1S, R2S = SCR[:, 3072:4096], SCR[:, 4096:5120]

            def rope_pairs(Wd, n_o, dst_dram, dst_head0):
                rope_free = r.n["sV"]
                detang = {}
                panel_consumed = [None] * n_o
                for o in range(n_o):
                    wp = WP[:, (o % 2) * 4096:(o % 2) * 4096 + 4096]
                    psem = "sA" if o % 2 == 0 else "sB"
                    if o >= 2:
                        r.wait("gpsimd", "sT", panel_consumed[o - 2])
                    else:
                        r.wait("gpsimd", "sT", r.n["sT"])
                    wa = dma("gpsimd", psem, wp.rearrange("d (k o) -> d k o", o=128),
                             Wd.rearrange("(k d) o -> d k o", d=128)[:, :,
                                          o * 128:(o + 1) * 128])
                    r.wait("tensor", psem, wa)
                    if o == 0:
                        r.wait("tensor", "sV", ln)
                    for tc in range(2):
                        bi = (2 * o + tc) % 4
                        bank_wait(bi)
                        for k in range(NC):
                            st, sp = (k == 0), (k == NC - 1)
                            tv0 = r.do("tensor",
                                       lambda E, S, bi=bi, k=k, tc=tc, wp=wp, st=st, sp=sp:
                                       E.matmul(P[bi][:], wp[:, k * 128:(k + 1) * 128],
                                                B2[:, k * T + tc * TH:k * T + tc * TH + TH],
                                                start=st, stop=sp),
                                       inc="sT" if sp else None)
                        tv = tv0
                        buf = RPA if o % 2 == 0 else RPB
                        if rope_free is not None and tc == 0:
                            r.wait("scalar", "sV", rope_free)
                        r.wait("scalar", "sT", tv)
                        sv = r.do("scalar", lambda E, S, buf=buf, tc=tc, bi=bi: E.activation(
                            out=buf[:, tc * TH:(tc + 1) * TH], in_=P[bi][:], func=AF.Copy),
                            inc="sS")
                        bank_free[bi] = ("sS", sv)
                    panel_consumed[o] = r.n["sT"]
                    if o % 2 == 1:
                        p = o // 2
                        r.wait("vector", "sS", r.n["sS"])
                        if p >= 1:
                            r.wait("vector", "sD", detang[p - 1])
                        else:
                            r.wait("vector", "sD", r.n["sD"])
                        r.do("vector", lambda E, S: E.tensor_tensor(
                            out=AB[:, 0:1024], in0=RPA[:], in1=COS2[:], op=OP.mult))
                        r.do("vector", lambda E, S: E.tensor_tensor(
                            out=AB[:, 1024:2048], in0=RPB[:], in1=SIN2[:], op=OP.mult))
                        r.do("vector", lambda E, S: E.drain())
                        r.do("vector", lambda E, S: E.tensor_tensor(
                            out=R1S[:], in0=AB[:, 0:1024], in1=AB[:, 1024:2048],
                            op=OP.subtract))
                        r.do("vector", lambda E, S: E.drain())
                        r.do("vector", lambda E, S: E.tensor_tensor(
                            out=AB[:, 0:1024], in0=RPA[:], in1=SIN2[:], op=OP.mult))
                        rope_free = r.do("vector", lambda E, S: E.tensor_tensor(
                            out=AB[:, 1024:2048], in0=RPB[:], in1=COS2[:], op=OP.mult),
                            inc="sV")
                        r.do("vector", lambda E, S: E.drain())
                        v_r2 = r.do("vector", lambda E, S: E.tensor_tensor(
                            out=R2S[:], in0=AB[:, 0:1024], in1=AB[:, 1024:2048],
                            op=OP.add), inc="sV")
                        r.wait("sync", "sV", v_r2)
                        dh = dst_dram.rearrange("(h d) t -> h d t", d=128)
                        h0 = dst_head0 + 2 * p
                        dma("sync", "sD", dh[h0, 0:64, :], R1S[0:64, :])
                        dma("sync", "sD", dh[h0 + 1, 0:64, :], R1S[64:128, :])
                        dma("sync", "sD", dh[h0, 64:128, :], R2S[0:64, :])
                        detang[p] = dma("sync", "sD", dh[h0 + 1, 64:128, :],
                                        R2S[64:128, :])
                return max(detang.values())

            qdone = rope_pairs(wQb, 32, QB_d, 0)
            agk = agin.rearrange("(r t) c -> r t c", r=2)[0] if False else None
            kdone = rope_pairs(wKb, 8, agin[0:1024, :], 0)
            r.wait("gpsimd", "sD", kdone)

            def ag_dst(tch, oh, src):
                return (agin[1024 + tch * 128:1024 + (tch + 1) * 128,
                             oh * 512:(oh + 1) * 512], src)
            vE, vF = tm_proj(wVb, ("sV", ln), ag_dst, bias_bf_row=None)

            r.wait("gpsimd", "sD", max(qdone, kdone))
            r.wait("gpsimd", "sE", vE)
            r.wait("gpsimd", "sF", vF)
            cc = r.do("gpsimd", lambda E, S: E.collective_compute(
                "AllGather", OP.bypass,
                replica_groups=[[0, 1], [2, 3], [4, 5], [6, 7]],
                ins=[agin_t.ap().opt()], outs=[agout_t.ap().opt()]), inc="sC")

            r.wait("sync", "sV", r.n["sV"])
            r.wait("sync", "sT", r.n["sT"])
            mload = dma("sync", "sD2", SCR[:, 0:8192].rearrange("p (s f) -> p s f", f=512),
                        cmask_d.rearrange("(s p) f -> p s f", p=128))
            KG = SCR[:, 8192:10240]
            VG = SCR[:, 10240:12288]
            ENT = [SCR[:, 12288 + i * 512:12288 + (i + 1) * 512] for i in range(3)]
            QH = SCR[:, 13824:14848]
            r.wait("sync", "sC", cc)
            ao_last = None
            kv_consumed = None
            for g in range(8):
                if kv_consumed is not None:
                    r.wait("sync", "sT", kv_consumed)
                dma("sync", "sD", KG[:, 0:512],
                    agout[g * 128:(g + 1) * 128, 0:512])
                dma("sync", "sD", KG[:, 1536:2048],
                    agout[g * 128:(g + 1) * 128, 512:1024])
                dma("sync", "sD", KG[:, 512:1024],
                    agout[2048 + g * 128:2048 + (g + 1) * 128, 0:512])
                dma("sync", "sD", KG[:, 1024:1536],
                    agout[2048 + g * 128:2048 + (g + 1) * 128, 512:1024])
                dma("sync", "sD", VG.rearrange("p (c d) -> p c d", d=128)[:, 0:4, :],
                    agout.rearrange("(rr t) c -> rr t c", rr=4)[1]
                    .rearrange("(c p) d2 -> p c d2", p=128)[:, :, g * 128:(g + 1) * 128]
                    if False else
                    agout[1024:1536, g * 128:(g + 1) * 128]
                    .rearrange("(c p) d -> p c d", p=128))
                dma("sync", "sD", VG.rearrange("p (c d) -> p c d", d=128)[:, 4:12, :],
                    agout[3072:4096, g * 128:(g + 1) * 128]
                    .rearrange("(c p) d -> p c d", p=128))
                kvd = dma("sync", "sD", VG.rearrange("p (c d) -> p c d", d=128)[:, 12:16, :],
                          agout[1536:2048, g * 128:(g + 1) * 128]
                          .rearrange("(c p) d -> p c d", p=128))
                for j4 in range(4):
                    h = 4 * g + j4
                    r.wait("sync", "sT", r.n["sT"])
                    qh = dma("sync", "sD", QH, QB_d[h * 128:(h + 1) * 128, :])
                    r.wait("tensor", "sD", max(kvd, qh))
                    r.wait("tensor", "sD2", mload)
                    for s in range(2):
                        nk = 8 if s == 0 else 16
                        ent_free = [None, None, None]
                        bank_wait(4)
                        bank_wait(6)
                        for c in range(nk):
                            sbank = P[c % 2]
                            bank_wait(c % 2)
                            mm = r.do("tensor", lambda E, S, c=c, s=s, sbank=sbank: E.matmul(
                                sbank[:], KG[:, c * 128:(c + 1) * 128],
                                QH[:, s * TH:(s + 1) * TH], start=True, stop=True),
                                inc="sT")
                            eb = ENT[c % 3]
                            if ent_free[c % 3] is not None:
                                r.wait("scalar", "sT", ent_free[c % 3])
                            r.wait("scalar", "sT", mm)
                            s_e = r.do("scalar", lambda E, S, eb=eb, sbank=sbank:
                                       E.activation(out=eb, in_=sbank[:], func=AF.Exp,
                                                    scale=SCALE, bias=np.float32(BSHIFT)),
                                       inc="sS")
                            bank_free[c % 2] = ("sS", s_e)
                            masked = (s == 0) or (c >= 8)
                            if masked:
                                slot = c if s == 0 else (8 + c - 8)
                                mk = SCR[:, slot * 512:(slot + 1) * 512]
                                r.wait("vector", "sS", s_e)
                                env = r.do("vector", lambda E, S, eb=eb, mk=mk:
                                           E.tensor_tensor(out=eb, in0=mk, in1=eb,
                                                           op=OP.mult), inc="sV")
                                en_sem = ("sV", env)
                            else:
                                en_sem = ("sS", s_e)
                            r.wait("tensor", en_sem[0], en_sem[1])
                            st, sp = (c == 0), (c == nk - 1)
                            r.do("tensor", lambda E, S, eb=eb, st=st, sp=sp: E.matmul(
                                p4[0:1, :], ONES[:, 0:1], eb, start=st, stop=sp))
                            ent_free[c % 3] = r.do(
                                "tensor", lambda E, S, eb=eb, c=c, st=st, sp=sp: E.matmul(
                                    p6[:], VG[:, c * 128:(c + 1) * 128], eb,
                                    start=st, stop=sp), inc="sT")
                        t_pv = ent_free[(nk - 1) % 3]
                        r.wait("scalar", "sT", t_pv)
                        s_l = r.do("scalar", lambda E, S: E.activation(
                            out=ST[0:1, 0:512], in_=p4[0:1, :], func=AF.Ln), inc="sS")
                        bank_free[4] = ("sS", s_l)
                        bank_wait(7)
                        r.wait("tensor", "sS", s_l)
                        t_b = r.do("tensor", lambda E, S: E.matmul(
                            p7[:], ONESF[:], ST[0:1, 0:512], start=True, stop=True), inc="sT")
                        r.wait("scalar", "sT", t_b)
                        s_r = r.do("scalar", lambda E, S, s=s: E.activation(
                            out=RB[:, s * 512:(s + 1) * 512], in_=p7[:], func=AF.Exp,
                            scale=-1.0), inc="sS")
                        bank_free[7] = ("sS", s_r)
                        r.wait("vector", "sS", s_r)
                        c0 = h * T + s * TH
                        ao_last = r.do("vector", lambda E, S, c0=c0, s=s: E.tensor_tensor(
                            out=B2[:, c0:c0 + TH], in0=p6[:], in1=RB[:, s * 512:(s + 1) * 512],
                            op=OP.mult), inc="sV")
                        bank_free[6] = ("sV", ao_last)
                kv_consumed = r.n["sT"]
            fm_proj(woB, NC, ("sV", ao_last), sink_resid())
            return r.n["sV"]

        v = sub1()
        if debug_tap == 1:
            r.wait("gpsimd", "sV", v)
            dma("gpsimd", "sW", dbg.rearrange("(c d) t -> d c t", d=128),
                X[:].rearrange("d (c t) -> d c t", t=T))
        if n_sub >= 2:
            v = sub_ffn(v)
            if debug_tap == 2:
                r.wait("gpsimd", "sV", v)
                dma("gpsimd", "sW", dbg.rearrange("(c d) t -> d c t", d=128),
                    X[:].rearrange("d (c t) -> d c t", t=T))
        if n_sub >= 3:
            v = sub3(v)
            if debug_tap == 3:
                r.wait("gpsimd", "sV", v)
                dma("gpsimd", "sW", dbg.rearrange("(c d) t -> d c t", d=128),
                    X[:].rearrange("d (c t) -> d c t", t=T))
        if n_sub >= 4:
            v = sub_ffn(v)

        r.wait("gpsimd", "sV", v)
        fin = dma("gpsimd", "sW", out.rearrange("(c d) t -> d c t", d=128),
                  X[:].rearrange("d (c t) -> d c t", t=T))
        r.wait("gpsimd", "sW", fin)

        @block.sync
        def _(E):
            for f in r.ops["sync"]:
                f(E, S)

        @block.gpsimd
        def _(E):
            for f in r.ops["gpsimd"]:
                f(E, S)

        @block.scalar
        def _(E):
            for f in r.ops["scalar"]:
                f(E, S)

        @block.vector
        def _(E):
            for f in r.ops["vector"]:
                f(E, S)

        @block.tensor
        def _(E):
            for f in r.ops["tensor"]:
                f(E, S)

    return nc



def _prep_shared(i):
    f32 = np.float32
    sh = {}
    sh["wqT"] = np.ascontiguousarray(i["L0_attn_a_Wq"].T.astype(f32))
    sh["wkT"] = np.ascontiguousarray(i["L0_attn_a_Wk"].T.astype(f32))
    sh["wvT"] = np.ascontiguousarray(i["L0_attn_a_Wv"].T.astype(f32))
    sh["woT"] = np.ascontiguousarray(i["L0_attn_a_Wo"].T.astype(f32))
    sh["bq_t"] = np.ascontiguousarray(i["L0_attn_a_bq"].reshape(32, 128).astype(f32))
    sh["bk_t"] = np.ascontiguousarray(i["L0_attn_a_bk"].reshape(32, 128).astype(f32))
    sh["bv_d"] = np.ascontiguousarray(i["L0_attn_a_bv"].astype(f32))
    sh["bo_d"] = np.ascontiguousarray(i["L0_attn_a_bo"].astype(f32))
    wqkv = i["L0_attn_b_w_qkv"].astype(f32)
    sh["wQb"] = np.ascontiguousarray(wqkv[:, 0:4096][:, _pair_perm(32)])
    sh["wKb"] = np.ascontiguousarray(wqkv[:, 4096:5120][:, _pair_perm(8)])
    sh["wVb"] = np.ascontiguousarray(wqkv[:, 5120:6144])
    sh["woB"] = np.ascontiguousarray(i["L0_attn_b_w_o"].astype(f32))
    sh["wbaseT"] = np.ascontiguousarray(i["L0_ffn_W_base"].T.astype(f32))
    sh["bbase_d"] = np.ascontiguousarray(i["L0_ffn_b_base"].astype(f32))
    sh["wrouT"] = np.ascontiguousarray(i["L0_ffn_W_router"].T.astype(f32))
    sh["brou"] = np.ascontiguousarray(i["L0_ffn_b_router"].reshape(8, 1).astype(f32))
    sh["a_all"] = np.ascontiguousarray(
        i["L0_ffn_lora_A"].transpose(1, 0, 2).reshape(4096, 128).astype(f32))
    sh["b_all"] = np.ascontiguousarray(i["L0_ffn_lora_B"].reshape(128, 4096).astype(f32))
    sh["maskil_d"] = np.ascontiguousarray(
        ((np.arange(128)[:, None] % 4) == (np.arange(128)[None, :] % 4)).astype(BF))
    sh["id32_d"] = np.eye(128, dtype=f32)
    sh["ones_d"] = np.ones((128, 512), BF)
    sh["onesf_d"] = np.ones((1, 128), f32)
    pr = np.zeros((8, 128), np.float32)
    for e in range(8):
        pr[e, e * 16:(e + 1) * 16] = 1.0
    sh["prep_d"] = np.ascontiguousarray(pr.astype(BF))
    return sh


def kernel(**inputs):
    x = np.asarray(inputs["L0_attn_a_x"], np.float32)
    sh = _prep_shared(inputs)
    nc = build_program(n_sub=N_SUB, debug_tap=DEBUG_TAP)
    in_maps = []
    for c in range(8):
        b = c // 2
        pos = _core_positions(c)
        cos2, sin2 = _cos_sin(pos)
        m = dict(sh)
        m["x_in"] = np.ascontiguousarray(x[b][pos].T)
        m["cos_d"] = cos2
        m["sin_d"] = sin2
        m["cmask_d"] = _causal_masks(c)
        in_maps.append(m)
    res = run_bass_kernel_spmd(nc, in_maps, core_ids=list(range(8)))
    outp = np.empty((4, 2048, 4096), np.float32)
    for c in range(8):
        b = c // 2
        pos = _core_positions(c)
        outp[b][pos] = res.results[c]["out"].T
    if DEBUG_TAP is not None:
        kernel._dbg = [res.results[c]["dbg"].T for c in range(8)]
    return outp


# version 02
# speedup vs baseline: 1.0426x; 1.0426x over previous
import numpy as np
import ml_dtypes
import concourse.bass as bass
import concourse.mybir as mybir
from concourse.alu_op_type import AluOpType as OP
from concourse.bass_utils import run_bass_kernel_spmd

F32 = mybir.dt.float32
BF16 = mybir.dt.bfloat16
AF = mybir.ActivationFunctionType
BF = ml_dtypes.bfloat16

D = 4096
T = 1024
NC = 32
TH = 512
HD = 128
SCALE = float(np.float32(1.0) / np.float32(np.sqrt(np.float32(HD))))
EXPC = float(np.exp(np.float32(0.001)))
BSHIFT = -24.0
LNEPS = 1e-5

N_SUB = 4
DEBUG_TAP = None

SEMS = ("sW", "sD", "sD2", "sT", "sV", "sS", "sC", "sA", "sB",
        "sM1", "sM2", "sM3", "sM4", "sE", "sF")



def _core_positions(c):
    if c % 2 == 0:
        return np.concatenate([np.arange(0, 512), np.arange(1536, 2048)])
    return np.arange(512, 1536)


def _pair_perm(n_heads):
    perm = []
    for p in range(n_heads // 2):
        for h in (2 * p, 2 * p + 1):
            perm.extend((h * 128 + np.arange(0, 128, 2)).tolist())
        for h in (2 * p, 2 * p + 1):
            perm.extend((h * 128 + np.arange(1, 128, 2)).tolist())
    return np.array(perm)


def _cos_sin(pos):
    inv = (1.0 / 10000.0 ** (np.arange(0, HD, 2, dtype=np.float32) / HD)).astype(np.float32)
    ang = (pos.astype(np.float32)[:, None] * inv[None, :]).astype(np.float32)
    cos = np.cos(ang).astype(np.float32).T
    sin = np.sin(ang).astype(np.float32).T
    return (np.ascontiguousarray(np.concatenate([cos, cos], 0)),
            np.ascontiguousarray(np.concatenate([sin, sin], 0)))


def _causal_masks(c):
    qpos = [0, 1536] if c % 2 == 0 else [512, 1024]
    tiles = np.zeros((16, 128, 512), np.float32)
    kk = np.arange(128)[:, None]
    qq = np.arange(512)[None, :]
    i = 0
    for s in (0, 1):
        for ch in (range(0, 8) if s == 0 else range(8, 16)):
            tiles[i] = ((128 * ch + kk) <= (qpos[s] + qq)).astype(np.float32)
            i += 1
    return np.ascontiguousarray(tiles.reshape(16 * 128, 512).astype(BF))



class R:
    def __init__(self):
        self.ops = {e: [] for e in ("sync", "gpsimd", "scalar", "vector", "tensor")}
        self.n = {s: 0 for s in SEMS}

    def wait(self, eng, sem, val):
        if val is None or val <= 0:
            return
        def f(E, S, sem=sem, val=val):
            E.wait_ge(S[sem], val)
        self.ops[eng].append(f)

    def do(self, eng, build, inc=None, k=1):
        tgt = None
        if inc is not None:
            self.n[inc] += k
            tgt = self.n[inc]
        def f(E, S, build=build, inc=inc, k=k):
            ins = build(E, S)
            if inc is not None:
                ins.then_inc(S[inc], k)
        self.ops[eng].append(f)
        return tgt



def build_program(n_sub=4, debug_tap=None):
    nc = bass.Bass("TRN2", target_bir_lowering=False)
    dt = lambda n, s, ty, k="ExternalInput": nc.dram_tensor(n, s, ty, kind=k).ap()

    x_in = dt("x_in", [D, T], F32)
    wqT = dt("wqT", [D, D], F32)
    wkT = dt("wkT", [D, D], F32)
    wvT = dt("wvT", [D, D], F32)
    woT = dt("woT", [D, D], F32)
    bq_t = dt("bq_t", [32, 128], F32)
    bk_t = dt("bk_t", [32, 128], F32)
    bo_t = dt("bo_t", [32, 128], F32)
    bbase_t = dt("bbase_t", [32, 128], F32)
    bv_d = dt("bv_d", [D], F32)
    bo_d = dt("bo_d", [D], F32)
    wQb = dt("wQb", [D, 4096], F32)
    wKb = dt("wKb", [D, 1024], F32)
    wVb = dt("wVb", [D, 1024], F32)
    woB = dt("woB", [D, D], F32)
    wbaseT = dt("wbaseT", [D, D], F32)
    bbase_d = dt("bbase_d", [D], F32)
    wrouT = dt("wrouT", [D, 8], F32)
    brou = dt("brou", [8, 1], F32)
    a_all = dt("a_all", [D, 128], F32)
    b_all = dt("b_all", [128, D], F32)
    cos_d = dt("cos_d", [128, T], F32)
    sin_d = dt("sin_d", [128, T], F32)
    cmask_d = dt("cmask_d", [16 * 128, 512], BF16)
    maskil_d = dt("maskil_d", [128, 128], BF16)
    id32_d = dt("id32_d", [128, 128], F32)
    ones_d = dt("ones_d", [128, 512], BF16)
    onesf_d = dt("onesf_d", [1, 128], F32)
    prep_d = dt("prep_d", [8, 128], BF16)

    out = dt("out", [D, T], F32, "ExternalOutput")
    dbg = dt("dbg", [D, T], F32, "ExternalOutput") if debug_tap is not None else None

    QA_d = dt("QA_d", [D, T], BF16, "Internal")
    KA_d = dt("KA_d", [D, T], BF16, "Internal")
    VA_d = dt("VA_d", [T * 32, 128], BF16, "Internal")
    QB_d = dt("QB_d", [D, T], BF16, "Internal")
    agin_t = nc.dram_tensor("agin", [2048, 1024], BF16, kind="Internal")
    agout_t = nc.dram_tensor("agout", [4096, 1024], BF16, kind="Internal")
    agin = agin_t.ap()
    agout = agout_t.ap()

    r = R()

    from contextlib import ExitStack
    with ExitStack() as _es:
        en = _es.enter_context
        X = en(nc.sbuf_tensor("X", [128, NC * T], BF16))
        B2 = en(nc.sbuf_tensor("B2", [128, NC * T], BF16))
        SCR = en(nc.sbuf_tensor("SCR", [128, 16384], BF16))
        WP = en(nc.sbuf_tensor("WP", [128, 8192], BF16))
        SP = en(nc.sbuf_tensor("SP", [128, 1024], BF16))
        AB = en(nc.sbuf_tensor("AB", [128, 2048], F32))
        RB = en(nc.sbuf_tensor("RB", [128, 1024], F32))
        ST = en(nc.sbuf_tensor("ST", [128, 1024], F32))
        SMX = en(nc.sbuf_tensor("SMX", [128, 384], F32))
        CB = en(nc.sbuf_tensor("CB", [128, 192], F32))
        COS2 = en(nc.sbuf_tensor("COS2", [128, T], F32))
        SIN2 = en(nc.sbuf_tensor("SIN2", [128, T], F32))
        MIL = en(nc.sbuf_tensor("MIL", [128, 128], BF16))
        ID32 = en(nc.sbuf_tensor("ID32", [128, 128], F32))
        ONES = en(nc.sbuf_tensor("ONES", [128, 512], BF16))
        ONESF = en(nc.sbuf_tensor("ONESF", [1, 128], F32))
        PREP = en(nc.sbuf_tensor("PREP", [8, 128], BF16))
        p0 = en(nc.psum_tensor("p0", [128, 512], F32))
        p1 = en(nc.psum_tensor("p1", [128, 512], F32))
        p2 = en(nc.psum_tensor("p2", [128, 512], F32))
        p3 = en(nc.psum_tensor("p3", [128, 512], F32))
        p4 = en(nc.psum_tensor("p4", [128, 512], F32))
        p5 = en(nc.psum_tensor("p5", [128, 512], F32))
        p6 = en(nc.psum_tensor("p6", [128, 512], F32))
        p7 = en(nc.psum_tensor("p7", [128, 512], F32))
        hW = en(nc.semaphore("sW"))
        hD = en(nc.semaphore("sD"))
        hD2 = en(nc.semaphore("sD2"))
        hT = en(nc.semaphore("sT"))
        hV = en(nc.semaphore("sV"))
        hS = en(nc.semaphore("sS"))
        hC = en(nc.semaphore("sC"))
        hA = en(nc.semaphore("sA"))
        hB = en(nc.semaphore("sB"))
        hM1 = en(nc.semaphore("sM1"))
        hM2 = en(nc.semaphore("sM2"))
        hM3 = en(nc.semaphore("sM3"))
        hM4 = en(nc.semaphore("sM4"))
        hE = en(nc.semaphore("sE"))
        hF = en(nc.semaphore("sF"))
        block = en(nc.Block())
        S = {"sW": hW, "sD": hD, "sD2": hD2, "sT": hT, "sV": hV, "sS": hS, "sC": hC,
             "sA": hA, "sB": hB, "sM1": hM1, "sM2": hM2, "sM3": hM3, "sM4": hM4,
             "sE": hE, "sF": hF}
        P = [p0, p1, p2, p3, p4, p5, p6, p7]
        STbf = ST[:].bitcast(BF16)
        ABbf = AB[:].bitcast(BF16)

        bank_free = [None] * 8

        sp_state = {"last": ("sD", 0)}

        def bank_wait(i, eng="tensor"):
            if bank_free[i] is not None:
                sem, val = bank_free[i]
                r.wait(eng, sem, val)
                bank_free[i] = None

        def dma(eng, sem, outap, inap):
            return r.do(eng, lambda E, S, o=outap, i=inap: E.dma_start(out=o, in_=i),
                        inc=sem, k=16)

        dma("sync", "sD", COS2[:], cos_d)
        dma("sync", "sD", SIN2[:], sin_d)
        dma("sync", "sD", MIL[:], maskil_d)
        dma("sync", "sD", ID32[:], id32_d)
        dma("sync", "sD", ONES[:], ones_d)
        dma("sync", "sD", ONESF[:], onesf_d)
        dma("sync", "sD", PREP[:], prep_d)
        dma("sync", "sD", CB[0:8, 128:129], brou)
        dma("sync", "sD", RB[0:32, 0:128], bq_t)
        dma("sync", "sD", RB[0:32, 128:256], bk_t)
        dma("sync", "sD", RB[0:32, 256:384], bo_t)
        v_setup = dma("sync", "sD", RB[0:32, 384:512], bbase_t)

        dma("gpsimd", "sW", X[:].rearrange("d (c t) -> d c t", t=T),
            x_in.rearrange("(c d) t -> d c t", d=128))
        w_xload = r.n["sW"]

        r.wait("tensor", "sD", v_setup)
        r.do("tensor", lambda E, S: E.matmul(p7[:, 0:32], RB[0:32, 0:128],
                                             ID32[0:32, 0:32], start=True, stop=True))
        r.do("tensor", lambda E, S: E.matmul(p7[:, 32:64], RB[0:32, 128:256],
                                             ID32[0:32, 0:32], start=True, stop=True))
        r.do("tensor", lambda E, S: E.matmul(p7[:, 64:96], RB[0:32, 256:384],
                                             ID32[0:32, 0:32], start=True, stop=True))
        t_bias = r.do("tensor", lambda E, S: E.matmul(p7[:, 96:128], RB[0:32, 384:512],
                                                      ID32[0:32, 0:32], start=True, stop=True),
                      inc="sT")
        r.wait("vector", "sT", t_bias)
        v_bias = r.do("vector", lambda E, S: E.tensor_copy(CB[:, 0:128], p7[:, 0:128]),
                      inc="sV")
        bank_free[7] = ("sV", v_bias)

        def emit_ln(xr_sem, xr_val):
            XSQ = [SCR[:, 0:1024], SCR[:, 1024:2048]]
            r.wait("scalar", xr_sem, xr_val)
            r.wait("tensor", xr_sem, xr_val)
            r.do("vector", lambda E, S: E.drain())
            for i in range(4):
                bank_wait(i)
            sq_done = [None] * NC
            mm_done = [None] * NC
            for j in range(NC):
                if j >= 2:
                    r.wait("scalar", "sT", mm_done[j - 2])
                sq_done[j] = r.do("scalar", lambda E, S, j=j: E.activation(
                    out=XSQ[j % 2], in_=X[:, j * T:(j + 1) * T], func=AF.Square), inc="sS")
                r.wait("tensor", "sS", sq_done[j])
                st, sp = (j == 0), (j == NC - 1)
                r.do("tensor", lambda E, S, j=j, st=st, sp=sp: E.matmul(
                    p0[:], ONES[:, 0:128], X[:, j * T:j * T + TH], start=st, stop=sp))
                r.do("tensor", lambda E, S, j=j, st=st, sp=sp: E.matmul(
                    p1[:], ONES[:, 0:128], X[:, j * T + TH:(j + 1) * T], start=st, stop=sp))
                r.do("tensor", lambda E, S, j=j, st=st, sp=sp: E.matmul(
                    p2[:], ONES[:, 0:128], XSQ[j % 2][:, 0:TH], start=st, stop=sp))
                mm_done[j] = r.do("tensor", lambda E, S, j=j, st=st, sp=sp: E.matmul(
                    p3[:], ONES[:, 0:128], XSQ[j % 2][:, TH:T], start=st, stop=sp), inc="sT")
            r.wait("scalar", "sT", mm_done[NC - 1])
            inv = 1.0 / D
            prev_t1 = prev_v2 = None
            for h in range(2):
                psx, psxx = (p0, p2) if h == 0 else (p1, p3)
                if h == 1:
                    r.wait("scalar", "sT", prev_t1)
                    r.wait("scalar", "sV", prev_v2)
                SCRf = SCR[:].bitcast(F32)
                sm = SCRf[0:1, 1024:1536]
                sm2 = SCRf[0:1, 1536:2048]
                sx3 = SCRf[0:1, 2048:2560]
                r.do("scalar", lambda E, S, psx=psx, sm=sm: E.activation(
                    out=sm, in_=psx[0:1, :], func=AF.Copy, scale=inv))
                r.do("scalar", lambda E, S, psx=psx, sm2=sm2: E.activation(
                    out=sm2, in_=psx[0:1, :], func=AF.Square, scale=inv))
                s1 = r.do("scalar", lambda E, S, psxx=psxx, sx3=sx3: E.activation(
                    out=sx3, in_=psxx[0:1, :], func=AF.Copy, scale=inv), inc="sS")
                r.wait("vector", "sS", s1)
                v1 = r.do("vector", lambda E, S, sm2=sm2, sx3=sx3: E.scalar_tensor_tensor(
                    out=sm2, in0=sx3, scalar=LNEPS, in1=sm2,
                    op0=OP.add, op1=OP.subtract), inc="sV")
                r.wait("scalar", "sV", v1)
                r.do("scalar", lambda E, S, sm2=sm2, sx3=sx3: E.activation(
                    out=sx3, in_=sm2, func=AF.Ln))
                r.do("scalar", lambda E, S: E.drain())
                s2 = r.do("scalar", lambda E, S, sx3=sx3: E.activation(
                    out=ST[0:1, 0:512], in_=sx3, func=AF.Exp, scale=-0.5),
                    inc="sS")
                r.wait("vector", "sS", s2)
                v2 = r.do("vector", lambda E, S, sm=sm: E.scalar_tensor_tensor(
                    out=ST[0:1, 512:1024], in0=sm, scalar=-1.0,
                    in1=ST[0:1, 0:512], op0=OP.mult, op1=OP.mult), inc="sV")
                bank_wait(4)
                bank_wait(5)
                r.wait("tensor", "sV", v2)
                r.do("tensor", lambda E, S: E.matmul(p4[:], ONESF[:], ST[0:1, 0:512],
                                                     start=True, stop=True))
                t1 = r.do("tensor", lambda E, S: E.matmul(p5[:], ONESF[:],
                                                          ST[0:1, 512:1024],
                                                          start=True, stop=True), inc="sT")
                r.wait("vector", "sT", t1)
                r.do("vector", lambda E, S, h=h: E.tensor_copy(AB[:, h * TH:(h + 1) * TH],
                                                               p4[:]))
                v3 = r.do("vector", lambda E, S, h=h: E.tensor_copy(
                    AB[:, 1024 + h * TH:1024 + (h + 1) * TH], p5[:]), inc="sV")
                bank_free[4] = ("sV", v3)
                bank_free[5] = ("sV", v3)
                prev_t1, prev_v2 = t1, v2
            for i, v in ((0, None), (1, None), (2, None), (3, None)):
                bank_free[i] = ("sS", r.n["sS"])
            r.do("vector", lambda E, S: E.drain())
            last = None
            for j in range(NC):
                for h in range(2):
                    c0 = j * T + h * TH
                    r.do("vector", lambda E, S, c0=c0, h=h: E.tensor_tensor(
                        out=RB[:, h * TH:(h + 1) * TH], in0=X[:, c0:c0 + TH],
                        in1=AB[:, h * TH:(h + 1) * TH], op=OP.mult))
                r.do("vector", lambda E, S: E.drain())
                for h in range(2):
                    c0 = j * T + h * TH
                    last = r.do("vector", lambda E, S, c0=c0, h=h: E.tensor_tensor(
                        out=B2[:, c0:c0 + TH], in0=RB[:, h * TH:(h + 1) * TH],
                        in1=AB[:, 1024 + h * TH:1024 + (h + 1) * TH], op=OP.add), inc="sV")
                r.do("vector", lambda E, S: E.drain())
            return last

        def fm_proj(Wd, n_o, rhs_ready, sink, bias_row=None, extra=None,
                    banks=(0, 1, 2, 3)):
            panel_consumed = [None] * n_o
            for o in range(n_o):
                wp = WP[:, (o % 2) * 4096:(o % 2) * 4096 + 4096]
                psem = "sA" if o % 2 == 0 else "sB"
                if o >= 2:
                    r.wait("gpsimd", "sT", panel_consumed[o - 2])
                else:
                    r.wait("gpsimd", "sT", r.n["sT"])
                wa = dma("gpsimd", psem, wp.rearrange("d (k o) -> d k o", o=128),
                         Wd.rearrange("(k d) o -> d k o", d=128)[:, :, o * 128:(o + 1) * 128])
                r.wait("tensor", psem, wa)
                if o == 0:
                    r.wait("tensor", rhs_ready[0], rhs_ready[1])
                for tc in range(2):
                    bi = banks[(2 * o + tc) % len(banks)]
                    bank_wait(bi)
                    for k in range(NC):
                        st = (k == 0)
                        last = (k == NC - 1) and bias_row is None and extra is None
                        tv0 = r.do("tensor",
                                   lambda E, S, bi=bi, k=k, tc=tc, wp=wp, st=st, sp=last:
                                   E.matmul(P[bi][:], wp[:, k * 128:(k + 1) * 128],
                                            B2[:, k * T + tc * TH:k * T + tc * TH + TH],
                                            start=st, stop=sp),
                                   inc="sT" if last else None)
                        if last:
                            tv = tv0
                    if bias_row is not None:
                        last = extra is None
                        tv0 = r.do("tensor",
                                   lambda E, S, bi=bi, o=o, br=bias_row, sp=last:
                                   E.matmul(P[bi][:], br[0:1, o * 128:(o + 1) * 128],
                                            ONES[0:1, 0:TH], start=False, stop=sp),
                                   inc="sT" if last else None)
                        if last:
                            tv = tv0
                    if extra is not None:
                        tv = extra(o, tc, bi)
                    bank_free[bi] = sink(o, tc, bi, tv)
                panel_consumed[o] = r.n["sT"]

        def sink_spill(dram, bias_col=None):
            spill_done = {}
            def sink(o, tc, bi, tv):
                sb = SP[:, tc * TH:(tc + 1) * TH]
                if o == 0 and tc == 0:
                    r.wait("scalar", sp_state["last"][0], sp_state["last"][1])
                if o >= 1 and (o - 1) in spill_done and tc == 0:
                    r.wait("scalar", "sD", spill_done[o - 1])
                r.wait("scalar", "sT", tv)
                if bias_col is not None:
                    sv = r.do("scalar", lambda E, S, sb=sb, bi=bi, o=o: E.activation(
                        out=sb, in_=P[bi][:], func=AF.Identity,
                        bias=CB[:, bias_col + o:bias_col + o + 1]), inc="sS")
                else:
                    sv = r.do("scalar", lambda E, S, sb=sb, bi=bi: E.activation(
                        out=sb, in_=P[bi][:], func=AF.Copy), inc="sS")
                if tc == 1:
                    r.wait("sync", "sS", sv)
                    spill_done[o] = dma("sync", "sD", dram[o * 128:(o + 1) * 128, :],
                                        SP[:, 0:1024])
                    sp_state["last"] = ("sD", spill_done[o])
                return ("sS", sv)
            sink.spill_done = spill_done
            return sink

        def sink_resid(bias_col=None):
            def sink(o, tc, bi, tv):
                c0 = o * T + tc * TH
                if bias_col is None:
                    r.wait("vector", "sT", tv)
                    vv = r.do("vector", lambda E, S, c0=c0, bi=bi: E.tensor_tensor(
                        out=X[:, c0:c0 + TH], in0=P[bi][:], in1=X[:, c0:c0 + TH],
                        op=OP.add), inc="sV")
                    return ("sV", vv)
                slot = (2 * o + tc) % 2
                tmp = SCR[:, slot * 512:(slot + 1) * 512]
                if slot in sink.tmp_user:
                    r.wait("scalar", "sV", sink.tmp_user[slot])
                r.wait("scalar", "sT", tv)
                sv = r.do("scalar", lambda E, S, tmp=tmp, bi=bi, o=o: E.activation(
                    out=tmp, in_=P[bi][:], func=AF.Identity,
                    bias=CB[:, bias_col + o:bias_col + o + 1]), inc="sS")
                r.wait("vector", "sS", sv)
                vv = r.do("vector", lambda E, S, c0=c0, tmp=tmp: E.tensor_tensor(
                    out=X[:, c0:c0 + TH], in0=tmp, in1=X[:, c0:c0 + TH],
                    op=OP.add), inc="sV")
                sink.tmp_user[slot] = vv
                return ("sV", vv)
            sink.tmp_user = {}
            return sink

        def load_bias_row(dst_row, dd, n, after_sem, after_val, sem="sM1"):
            r.wait("gpsimd", after_sem, after_val)
            return dma("gpsimd", sem, dst_row[0:1, 0:n],
                       dd.rearrange("(a b) -> a b", a=1))

        def tm_proj(Wd, rhs_ready, dst_fn, n_oh=2, bias_bf_row=None):
            r.wait("tensor", rhs_ready[0], rhs_ready[1])
            tiles = [SCR[:, 2048:2560], SCR[:, 2560:3072]]
            half_done = {}
            prev_used = None
            for oh in range(n_oh):
                tile_used = [None] * NC
                for k in range(NC):
                    tsem = "sA" if k % 2 == 0 else "sB"
                    if k >= 2:
                        r.wait("gpsimd", "sT", tile_used[k - 2])
                    elif prev_used is not None:
                        r.wait("gpsimd", "sT", prev_used[NC - 2 + k])
                    else:
                        r.wait("gpsimd", "sT", r.n["sT"])
                    wa = dma("gpsimd", tsem, tiles[k % 2],
                             Wd.rearrange("(k d) o -> d k o", d=128)[:, k,
                                          oh * 512:(oh + 1) * 512])
                    r.wait("tensor", tsem, wa)
                    for tch in range(8):
                        if k == 0:
                            bank_wait(tch)
                        st = (k == 0)
                        sp = (k == NC - 1) and bias_bf_row is None
                        tu = r.do("tensor",
                                  lambda E, S, tch=tch, k=k, st=st, sp=sp, tl=tiles[k % 2]:
                                  E.matmul(P[tch][:],
                                           B2[:, k * T + tch * 128:k * T + (tch + 1) * 128],
                                           tl, start=st, stop=sp),
                                  inc="sT" if tch == 7 else None)
                    tile_used[k] = tu
                prev_used = tile_used
                for tch in range(8):
                    if bias_bf_row is not None:
                        tv = r.do("tensor", lambda E, S, tch=tch, oh=oh, br=bias_bf_row:
                                  E.matmul(P[tch][:], ONES[0:1, 0:128],
                                           br[0:1, oh * 512:(oh + 1) * 512],
                                           start=False, stop=True), inc="sT")
                    else:
                        tv = tile_used[NC - 1]
                    hsem = "sE" if oh % 2 == 0 else "sF"
                    if (oh % 2) in half_done:
                        r.wait("scalar", hsem, half_done[oh % 2])
                    elif oh < 2 and tch == 0:
                        r.wait("scalar", sp_state["last"][0], sp_state["last"][1])
                    r.wait("scalar", "sT", tv)
                    sv = r.do("scalar", lambda E, S, tch=tch, oh=oh: E.activation(
                        out=SP[:, (oh % 2) * TH:(oh % 2 + 1) * TH],
                        in_=P[tch][:], func=AF.Copy), inc="sS")
                    bank_free[tch] = ("sS", sv)
                    r.wait("sync", "sS", sv)
                    oap, iap = dst_fn(tch, oh, SP[:, (oh % 2) * TH:(oh % 2 + 1) * TH])
                    half_done[oh % 2] = dma("sync", hsem, oap, iap)
                    sp_state["last"] = (hsem, half_done[oh % 2])
            return (r.n["sE"], r.n["sF"])

        tap_points = {}

        def sub1():
            ln1 = emit_ln("sW", w_xload)
            skq = sink_spill(QA_d, bias_col=0)
            fm_proj(wqT, NC, ("sV", ln1), skq)
            skk = sink_spill(KA_d, bias_col=32)
            fm_proj(wkT, NC, ("sV", ln1), skk)
            bw = load_bias_row(ABbf, bv_d, D, "sV", ln1)
            r.wait("tensor", "sM1", bw)
            va_view = VA_d.rearrange("(tt g tp) d -> tt tp g d", g=32, tp=128)

            def va_dst(tch, oh, src):
                return (va_view[tch][:, 4 * oh:4 * oh + 4, :],
                        src.rearrange("p (g4 d) -> p g4 d", d=128))
            vE, vF = tm_proj(wvT, ("sV", ln1), va_dst, n_oh=8, bias_bf_row=ABbf)


            spills = max(max(skq.spill_done.values()), max(skk.spill_done.values()))

            QC, KC, VR, EN = (SCR[:, 0:4096], SCR[:, 4096:8192],
                              SCR[:, 8192:12288], SCR[:, 12288:16384])
            ao_last = None
            for tt in range(8):
                t0 = tt * 128
                r.wait("sync", "sT", r.n["sT"])
                if tt == 0:
                    r.wait("sync", "sD", spills)
                    r.wait("sync", "sE", vE)
                    r.wait("sync", "sF", vF)
                dma("sync", "sD", QC.rearrange("d (h t) -> d h t", t=128),
                    QA_d.rearrange("(h d) t -> d h t", d=128)[:, :, t0:t0 + 128])
                dma("sync", "sD", KC.rearrange("d (h t) -> d h t", t=128),
                    KA_d.rearrange("(h d) t -> d h t", d=128)[:, :, t0:t0 + 128])
                ld = dma("sync", "sD",
                         VR.rearrange("p (gp d) -> p gp d", d=128),
                         VA_d.rearrange("(tt g j gp) d -> tt (g j) gp d",
                                        tt=8, g=32, j=4)[tt])
                r.wait("tensor", "sD", ld)
                for gq in range(8):
                    sbank = P[gq % 2]
                    pvbank = P[5 + (gq % 2)]
                    bank_wait(gq % 2)
                    for gg in range(4):
                        g = gq * 4 + gg
                        t_s = r.do("tensor", lambda E, S, g=g, gg=gg, sbank=sbank: E.matmul(
                            sbank[:, gg * 128:(gg + 1) * 128], KC[:, g:4096:32],
                            QC[:, g:4096:32], start=True, stop=True),
                            inc="sT" if gg == 3 else None)
                    r.wait("scalar", "sT", t_s)
                    s_e = r.do("scalar", lambda E, S, gq=gq, sbank=sbank: E.activation(
                        out=EN[:, gq * 512:(gq + 1) * 512], in_=sbank[:], func=AF.Exp,
                        scale=SCALE), inc="sS")
                    r.wait("vector", "sS", s_e)
                    r.do("vector", lambda E, S, gq=gq, sbank=sbank: E.tensor_tensor(
                        out=sbank[:].rearrange("p (a b) -> p a b", b=128),
                        in0=MIL[:].unsqueeze(1).broadcast_to([128, 4, 128]),
                        in1=EN[:, gq * 512:(gq + 1) * 512].rearrange("p (a b) -> p a b",
                                                                     b=128),
                        op=OP.mult))
                    r.do("vector", lambda E, S: E.drain())
                    v_u = r.do("vector", lambda E, S, gq=gq, sbank=sbank:
                               E.scalar_tensor_tensor(
                                   out=EN[:, gq * 512:(gq + 1) * 512], in0=sbank[:],
                                   scalar=EXPC, in1=EN[:, gq * 512:(gq + 1) * 512],
                                   op0=OP.is_gt, op1=OP.mult), inc="sV")
                    bank_free[gq % 2] = ("sV", v_u)
                    bank_wait(4)
                    bank_wait(5 + (gq % 2))
                    r.wait("tensor", "sV", v_u)
                    for gg in range(4):
                        g = gq * 4 + gg
                        r.do("tensor", lambda E, S, g=g, gg=gg: E.matmul(
                            p4[:, gg * 128:(gg + 1) * 128], ONES[:, 0:128],
                            EN[:, g * 128:(g + 1) * 128], start=True, stop=True))
                        t_d = r.do("tensor", lambda E, S, g=g, gg=gg, pvbank=pvbank: E.matmul(
                            pvbank[:, gg * 128:(gg + 1) * 128], VR[:, g * 128:(g + 1) * 128],
                            EN[:, g * 128:(g + 1) * 128], start=True, stop=True),
                            inc="sT" if gg == 3 else None)
                    r.wait("scalar", "sT", t_d)
                    s_l = r.do("scalar", lambda E, S: E.activation(
                        out=ST[0:1, 0:512], in_=p4[0:1, :], func=AF.Ln), inc="sS")
                    bank_free[4] = ("sS", s_l)
                    bank_wait(7)
                    r.wait("tensor", "sS", s_l)
                    t_b = r.do("tensor", lambda E, S: E.matmul(
                        p7[:], ONESF[:], ST[0:1, 0:512], start=True, stop=True), inc="sT")
                    r.wait("scalar", "sT", t_b)
                    s_r = r.do("scalar", lambda E, S, gq=gq: E.activation(
                        out=RB[:, (gq % 2) * 512:(gq % 2) * 512 + 512], in_=p7[:],
                        func=AF.Exp, scale=-1.0), inc="sS")
                    bank_free[7] = ("sS", s_r)
                    r.wait("vector", "sS", s_r)
                    dst = B2[:].rearrange("p (h jj aa) -> p aa h jj", h=32, jj=32, aa=32)[
                        :, 4 * gq:4 * gq + 4, :, 4 * tt:4 * tt + 4]
                    src = pvbank[:].rearrange("p (a h j) -> p a h j", a=4, h=32)
                    rrv = RB[:, (gq % 2) * 512:(gq % 2) * 512 + 512].rearrange(
                        "p (a h j) -> p a h j", a=4, h=32)
                    ao_last = r.do("vector", lambda E, S, dst=dst, src=src, rrv=rrv:
                                   E.tensor_tensor(out=dst, in0=src, in1=rrv, op=OP.mult),
                                   inc="sV")
                    bank_free[5 + (gq % 2)] = ("sV", ao_last)
            fm_proj(woT, NC, ("sV", ao_last), sink_resid(bias_col=64))
            return r.n["sV"]

        def sub_ffn(prev_v):
            ln = emit_ln("sV", prev_v)
            ROUT = SCR[:, 0:256]
            r.wait("gpsimd", "sV", r.n["sV"])
            r.wait("gpsimd", "sT", r.n["sT"])
            rw = dma("gpsimd", "sM2", ROUT.rearrange("d (k e) -> d k e", e=8),
                     wrouT.rearrange("(k d) e -> d k e", d=128))
            r.wait("tensor", "sM2", rw)
            r.wait("tensor", "sV", ln)
            bank_wait(4)
            bank_wait(5)
            for tc in range(2):
                pb = p4 if tc == 0 else p5
                for k in range(NC):
                    st, sp = (k == 0), (k == NC - 1)
                    t_r = r.do("tensor",
                               lambda E, S, pb=pb, k=k, tc=tc, st=st, sp=sp: E.matmul(
                                   pb[0:8, :], ROUT[:, k * 8:(k + 1) * 8],
                                   B2[:, k * T + tc * TH:k * T + tc * TH + TH],
                                   start=st, stop=sp),
                               inc="sT" if (tc == 1 and k == NC - 1) else None)
            r.wait("scalar", "sT", t_r)
            for tc in range(2):
                pb = p4 if tc == 0 else p5
                s_lg = r.do("scalar", lambda E, S, pb=pb, tc=tc: E.activation(
                    out=RB[0:8, tc * TH:(tc + 1) * TH], in_=pb[0:8, :], func=AF.Identity,
                    bias=CB[0:8, 64:65]), inc="sS" if tc == 1 else None)
            bank_free[4] = ("sS", s_lg)
            bank_free[5] = ("sS", s_lg)
            bank_wait(6)
            r.wait("tensor", "sS", s_lg)
            for c in range(8):
                t_t = r.do("tensor", lambda E, S, c=c: E.matmul(
                    p6[:, c * 8:(c + 1) * 8], RB[0:8, c * 128:(c + 1) * 128],
                    ID32[0:8, 0:8], start=True, stop=True),
                    inc="sT" if c == 7 else None)
            r.wait("vector", "sT", t_t)
            v_lt = r.do("vector", lambda E, S: E.tensor_copy(SMX[:, 0:64], p6[:, 0:64]),
                        inc="sV")
            bank_free[6] = ("sV", v_lt)
            r.do("vector", lambda E, S: E.drain())

            LT = lambda e: SMX[:, e * 8:e * 8 + 8] if False else None
            cols = lambda b, e: SMX[:, b + e:b + 64:8]
            def vd(build):
                v = r.do("vector", build, inc="sV")
                r.do("vector", lambda E, S: E.drain())
                return v
            vd(lambda E, S: E.tensor_copy(SMX[:, 64:128], SMX[:, 0:64]))
            net = [(0, 1), (2, 3), (4, 5), (6, 7), (0, 2), (1, 3), (4, 6), (5, 7),
                   (1, 2), (5, 6), (0, 4), (3, 7), (1, 5), (2, 6), (1, 4), (3, 6),
                   (2, 4), (3, 5), (3, 4)]
            for (a, b) in net:
                ca, cb = cols(64, a), cols(64, b)
                tmp = SMX[:, 280:288]
                vd(lambda E, S, ca=ca, cb=cb, tmp=tmp: E.tensor_tensor(
                    out=tmp, in0=ca, in1=cb, op=OP.max))
                vd(lambda E, S, ca=ca, cb=cb: E.tensor_tensor(
                    out=cb, in0=ca, in1=cb, op=OP.min))
                vd(lambda E, S, ca=ca, tmp=tmp: E.tensor_copy(ca, tmp))
            vd(lambda E, S: E.tensor_copy(cols(128, 0), cols(64, 0)))
            for e in range(1, 8):
                vd(lambda E, S, e=e: E.tensor_tensor(
                    out=cols(128, e), in0=cols(128, e - 1), in1=cols(64, e), op=OP.add))
            for e in range(8):
                vd(lambda E, S, e=e: E.tensor_scalar(
                    out=SMX[:, 280:288], in0=cols(64, e), scalar1=float(e + 1),
                    scalar2=1.0, op0=OP.mult, op1=OP.add))
                vd(lambda E, S, e=e: E.tensor_tensor(
                    out=cols(192, e), in0=SMX[:, 280:288], in1=cols(128, e), op=OP.is_gt))
            vd(lambda E, S: E.tensor_tensor(
                out=SMX[:, 256:264], in0=cols(192, 0), in1=cols(192, 1), op=OP.add))
            for e in range(2, 8):
                vd(lambda E, S, e=e: E.tensor_tensor(
                    out=SMX[:, 256:264], in0=SMX[:, 256:264], in1=cols(192, e), op=OP.add))
            vd(lambda E, S: E.tensor_tensor(
                out=SMX[:, 264:272], in0=cols(192, 0), in1=cols(64, 0), op=OP.mult))
            for e in range(1, 8):
                vd(lambda E, S, e=e: E.tensor_tensor(
                    out=SMX[:, 280:288], in0=cols(192, e), in1=cols(64, e), op=OP.mult))
                vd(lambda E, S: E.tensor_tensor(
                    out=SMX[:, 264:272], in0=SMX[:, 264:272], in1=SMX[:, 280:288],
                    op=OP.add))
            v_kz = r.n["sV"]
            r.wait("scalar", "sV", v_kz)
            r.do("scalar", lambda E, S: E.activation(out=SMX[:, 280:288],
                                                     in_=SMX[:, 256:264], func=AF.Ln))
            r.do("scalar", lambda E, S: E.drain())
            s_rk = r.do("scalar", lambda E, S: E.activation(
                out=SMX[:, 256:264], in_=SMX[:, 280:288], func=AF.Exp, scale=-1.0),
                inc="sS")
            r.wait("vector", "sS", s_rk)
            vd(lambda E, S: E.scalar_tensor_tensor(
                out=SMX[:, 272:280], in0=SMX[:, 264:272], scalar=-1.0, in1=SMX[:, 256:264],
                op0=OP.add, op1=OP.mult))
            v_w = None
            for e in range(8):
                vd(lambda E, S, e=e: E.tensor_tensor(
                    out=SMX[:, 280:288], in0=cols(0, e), in1=SMX[:, 272:280],
                    op=OP.subtract))
                v_w = vd(lambda E, S, e=e: E.tensor_single_scalar(
                    out=cols(288, e), in_=SMX[:, 280:288], scalar=0.0, op=OP.max))

            WFM = SCR[0:8, 256:1280]
            bank_wait(4)
            bank_wait(5)
            r.wait("tensor", "sV", v_w)
            for c in range(8):
                pb = p4 if c < 4 else p5
                t_w = r.do("tensor", lambda E, S, c=c, pb=pb: E.matmul(
                    pb[0:8, (c % 4) * 128:(c % 4 + 1) * 128],
                    SMX[:, 288 + c * 8:288 + (c + 1) * 8], ID32[:], start=True, stop=True),
                    inc="sT" if c == 7 else None)
            r.wait("scalar", "sT", t_w)
            r.do("scalar", lambda E, S: E.activation(out=WFM[:, 0:512], in_=p4[0:8, :],
                                                     func=AF.Copy))
            s_wf = r.do("scalar", lambda E, S: E.activation(
                out=WFM[:, 512:1024], in_=p5[0:8, :], func=AF.Copy), inc="sS")
            bank_free[4] = ("sS", s_wf)
            bank_free[5] = ("sS", s_wf)
            WREP = SCR[:, 1280:2304]
            bank_wait(6)
            bank_wait(7)
            r.wait("tensor", "sS", s_wf)
            r.do("tensor", lambda E, S: E.matmul(p6[:], PREP[:], WFM[:, 0:512],
                                                 start=True, stop=True))
            t_wr = r.do("tensor", lambda E, S: E.matmul(p7[:], PREP[:], WFM[:, 512:1024],
                                                        start=True, stop=True), inc="sT")
            r.wait("scalar", "sT", t_wr)
            r.do("scalar", lambda E, S: E.activation(out=WREP[:, 0:512], in_=p6[:],
                                                     func=AF.Copy))
            s_wr = r.do("scalar", lambda E, S: E.activation(
                out=WREP[:, 512:1024], in_=p7[:], func=AF.Copy), inc="sS")
            bank_free[6] = ("sS", s_wr)
            bank_free[7] = ("sS", s_wr)

            APAN = SCR[:, 2304:6400]
            aw = dma("gpsimd", "sM3", APAN.rearrange("d (k r) -> d k r", r=128),
                     a_all.rearrange("(k d) r -> d k r", d=128))
            G = SCR[:, 6400:7424]
            bank_wait(2)
            bank_wait(3)
            r.wait("tensor", "sM3", aw)
            for tc in range(2):
                pb = p2 if tc == 0 else p3
                for k in range(NC):
                    st, sp = (k == 0), (k == NC - 1)
                    t_h = r.do("tensor",
                               lambda E, S, pb=pb, k=k, tc=tc, st=st, sp=sp: E.matmul(
                                   pb[:], APAN[:, k * 128:(k + 1) * 128],
                                   B2[:, k * T + tc * TH:k * T + tc * TH + TH],
                                   start=st, stop=sp),
                               inc="sT" if (tc == 1 and k == NC - 1) else None)
            r.wait("vector", "sT", t_h)
            r.wait("vector", "sS", s_wr)
            r.do("vector", lambda E, S: E.tensor_tensor(
                out=G[0:128, 0:512], in0=p2[:], in1=WREP[:, 0:512], op=OP.mult))
            v_g = r.do("vector", lambda E, S: E.tensor_tensor(
                out=G[0:128, 512:1024], in0=p3[:], in1=WREP[:, 512:1024], op=OP.mult),
                inc="sV")
            bank_free[2] = ("sV", v_g)
            bank_free[3] = ("sV", v_g)

            BPAN = SCR[:, 7424:11520]
            bw2 = dma("gpsimd", "sM4", BPAN.rearrange("p (k o) -> p k o", o=128),
                      b_all.rearrange("er (k o) -> er k o", o=128))
            r.wait("tensor", "sM4", bw2)
            r.wait("tensor", "sV", v_g)

            def extra(o, tc, bi):
                return r.do("tensor", lambda E, S, o=o, tc=tc, bi=bi: E.matmul(
                    P[bi][:], BPAN[:, o * 128:(o + 1) * 128], G[:, tc * TH:(tc + 1) * TH],
                    start=False, stop=True), inc="sT")
            fm_proj(wbaseT, NC, ("sV", v_g), sink_resid(bias_col=96), extra=extra)
            return r.n["sV"]

        def sub3(prev_v):
            ln = emit_ln("sV", prev_v)
            RPA, RPB = SCR[:, 0:1024], SCR[:, 1024:2048]
            R1S, R2S = SCR[:, 3072:4096], SCR[:, 4096:5120]

            def rope_pairs(Wd, n_o, dst_dram, dst_head0):
                rope_free = r.n["sV"]
                detang = {}
                panel_consumed = [None] * n_o
                for o in range(n_o):
                    wp = WP[:, (o % 2) * 4096:(o % 2) * 4096 + 4096]
                    psem = "sA" if o % 2 == 0 else "sB"
                    if o >= 2:
                        r.wait("gpsimd", "sT", panel_consumed[o - 2])
                    else:
                        r.wait("gpsimd", "sT", r.n["sT"])
                    wa = dma("gpsimd", psem, wp.rearrange("d (k o) -> d k o", o=128),
                             Wd.rearrange("(k d) o -> d k o", d=128)[:, :,
                                          o * 128:(o + 1) * 128])
                    r.wait("tensor", psem, wa)
                    if o == 0:
                        r.wait("tensor", "sV", ln)
                    for tc in range(2):
                        bi = (2 * o + tc) % 4
                        bank_wait(bi)
                        for k in range(NC):
                            st, sp = (k == 0), (k == NC - 1)
                            tv0 = r.do("tensor",
                                       lambda E, S, bi=bi, k=k, tc=tc, wp=wp, st=st, sp=sp:
                                       E.matmul(P[bi][:], wp[:, k * 128:(k + 1) * 128],
                                                B2[:, k * T + tc * TH:k * T + tc * TH + TH],
                                                start=st, stop=sp),
                                       inc="sT" if sp else None)
                        tv = tv0
                        buf = RPA if o % 2 == 0 else RPB
                        if rope_free is not None and tc == 0:
                            r.wait("scalar", "sV", rope_free)
                        r.wait("scalar", "sT", tv)
                        sv = r.do("scalar", lambda E, S, buf=buf, tc=tc, bi=bi: E.activation(
                            out=buf[:, tc * TH:(tc + 1) * TH], in_=P[bi][:], func=AF.Copy),
                            inc="sS")
                        bank_free[bi] = ("sS", sv)
                    panel_consumed[o] = r.n["sT"]
                    if o % 2 == 1:
                        p = o // 2
                        r.wait("vector", "sS", r.n["sS"])
                        if p >= 1:
                            r.wait("vector", "sD", detang[p - 1])
                        else:
                            r.wait("vector", "sD", r.n["sD"])
                        r.do("vector", lambda E, S: E.tensor_tensor(
                            out=AB[:, 0:1024], in0=RPA[:], in1=COS2[:], op=OP.mult))
                        r.do("vector", lambda E, S: E.tensor_tensor(
                            out=AB[:, 1024:2048], in0=RPB[:], in1=SIN2[:], op=OP.mult))
                        r.do("vector", lambda E, S: E.drain())
                        r.do("vector", lambda E, S: E.tensor_tensor(
                            out=R1S[:], in0=AB[:, 0:1024], in1=AB[:, 1024:2048],
                            op=OP.subtract))
                        r.do("vector", lambda E, S: E.drain())
                        r.do("vector", lambda E, S: E.tensor_tensor(
                            out=AB[:, 0:1024], in0=RPA[:], in1=SIN2[:], op=OP.mult))
                        rope_free = r.do("vector", lambda E, S: E.tensor_tensor(
                            out=AB[:, 1024:2048], in0=RPB[:], in1=COS2[:], op=OP.mult),
                            inc="sV")
                        r.do("vector", lambda E, S: E.drain())
                        v_r2 = r.do("vector", lambda E, S: E.tensor_tensor(
                            out=R2S[:], in0=AB[:, 0:1024], in1=AB[:, 1024:2048],
                            op=OP.add), inc="sV")
                        r.wait("sync", "sV", v_r2)
                        dh = dst_dram.rearrange("(h d) t -> h d t", d=128)
                        h0 = dst_head0 + 2 * p
                        dma("sync", "sD", dh[h0, 0:64, :], R1S[0:64, :])
                        dma("sync", "sD", dh[h0 + 1, 0:64, :], R1S[64:128, :])
                        dma("sync", "sD", dh[h0, 64:128, :], R2S[0:64, :])
                        detang[p] = dma("sync", "sD", dh[h0 + 1, 64:128, :],
                                        R2S[64:128, :])
                return max(detang.values())

            qdone = rope_pairs(wQb, 32, QB_d, 0)
            agk = agin.rearrange("(r t) c -> r t c", r=2)[0] if False else None
            kdone = rope_pairs(wKb, 8, agin[0:1024, :], 0)
            r.wait("gpsimd", "sD", kdone)

            def ag_dst(tch, oh, src):
                return (agin[1024 + tch * 128:1024 + (tch + 1) * 128,
                             oh * 512:(oh + 1) * 512], src)
            vE, vF = tm_proj(wVb, ("sV", ln), ag_dst, bias_bf_row=None)

            r.wait("gpsimd", "sD", max(qdone, kdone))
            r.wait("gpsimd", "sE", vE)
            r.wait("gpsimd", "sF", vF)
            cc = r.do("gpsimd", lambda E, S: E.collective_compute(
                "AllGather", OP.bypass,
                replica_groups=[[0, 1], [2, 3], [4, 5], [6, 7]],
                ins=[agin_t.ap().opt()], outs=[agout_t.ap().opt()]), inc="sC")

            r.wait("sync", "sV", r.n["sV"])
            r.wait("sync", "sT", r.n["sT"])
            mload = dma("sync", "sD2", SCR[:, 0:8192].rearrange("p (s f) -> p s f", f=512),
                        cmask_d.rearrange("(s p) f -> p s f", p=128))
            KG = SCR[:, 8192:10240]
            VG = SCR[:, 10240:12288]
            ENT = [SCR[:, 12288 + i * 512:12288 + (i + 1) * 512] for i in range(3)]
            QHs = [SCR[:, 13824:14848], SCR[:, 14848:15872]]
            qh_user = {}
            r.wait("sync", "sC", cc)
            ao_last = None
            kv_consumed = None
            for g in range(8):
                if kv_consumed is not None:
                    r.wait("sync", "sT", kv_consumed)
                dma("sync", "sD", KG[:, 0:512],
                    agout[g * 128:(g + 1) * 128, 0:512])
                dma("sync", "sD", KG[:, 1536:2048],
                    agout[g * 128:(g + 1) * 128, 512:1024])
                dma("sync", "sD", KG[:, 512:1024],
                    agout[2048 + g * 128:2048 + (g + 1) * 128, 0:512])
                dma("sync", "sD", KG[:, 1024:1536],
                    agout[2048 + g * 128:2048 + (g + 1) * 128, 512:1024])
                dma("sync", "sD", VG.rearrange("p (c d) -> p c d", d=128)[:, 0:4, :],
                    agout.rearrange("(rr t) c -> rr t c", rr=4)[1]
                    .rearrange("(c p) d2 -> p c d2", p=128)[:, :, g * 128:(g + 1) * 128]
                    if False else
                    agout[1024:1536, g * 128:(g + 1) * 128]
                    .rearrange("(c p) d -> p c d", p=128))
                dma("sync", "sD", VG.rearrange("p (c d) -> p c d", d=128)[:, 4:12, :],
                    agout[3072:4096, g * 128:(g + 1) * 128]
                    .rearrange("(c p) d -> p c d", p=128))
                kvd = dma("sync", "sD", VG.rearrange("p (c d) -> p c d", d=128)[:, 12:16, :],
                          agout[1536:2048, g * 128:(g + 1) * 128]
                          .rearrange("(c p) d -> p c d", p=128))
                for j4 in range(4):
                    h = 4 * g + j4
                    hs = h % 2
                    hsem = "sE" if hs == 0 else "sF"
                    QH = QHs[hs]
                    if hs in qh_user:
                        r.wait("sync", "sT", qh_user[hs])
                    qh = dma("sync", hsem, QH, QB_d[h * 128:(h + 1) * 128, :])
                    r.wait("tensor", "sD", kvd)
                    r.wait("tensor", hsem, qh)
                    r.wait("tensor", "sD2", mload)
                    for s in range(2):
                        nk = 8 if s == 0 else 16
                        ent_free = [None, None, None]
                        bank_wait(4)
                        bank_wait(6)
                        for c in range(nk):
                            sbank = P[c % 2]
                            bank_wait(c % 2)
                            mm = r.do("tensor",
                                      lambda E, S, c=c, s=s, sbank=sbank, qhb=QH: E.matmul(
                                          sbank[:], KG[:, c * 128:(c + 1) * 128],
                                          qhb[:, s * TH:(s + 1) * TH], start=True, stop=True),
                                      inc="sT")
                            eb = ENT[c % 3]
                            if ent_free[c % 3] is not None:
                                r.wait("scalar", "sT", ent_free[c % 3])
                            r.wait("scalar", "sT", mm)
                            s_e = r.do("scalar", lambda E, S, eb=eb, sbank=sbank:
                                       E.activation(out=eb, in_=sbank[:], func=AF.Exp,
                                                    scale=SCALE, bias=np.float32(BSHIFT)),
                                       inc="sS")
                            bank_free[c % 2] = ("sS", s_e)
                            masked = (s == 0) or (c >= 8)
                            if masked:
                                slot = c if s == 0 else (8 + c - 8)
                                mk = SCR[:, slot * 512:(slot + 1) * 512]
                                r.wait("vector", "sS", s_e)
                                env = r.do("vector", lambda E, S, eb=eb, mk=mk:
                                           E.tensor_tensor(out=eb, in0=mk, in1=eb,
                                                           op=OP.mult), inc="sV")
                                en_sem = ("sV", env)
                            else:
                                en_sem = ("sS", s_e)
                            r.wait("tensor", en_sem[0], en_sem[1])
                            st, sp = (c == 0), (c == nk - 1)
                            r.do("tensor", lambda E, S, eb=eb, st=st, sp=sp: E.matmul(
                                p4[:], ONES[:, 0:128], eb, start=st, stop=sp))
                            ent_free[c % 3] = r.do(
                                "tensor", lambda E, S, eb=eb, c=c, st=st, sp=sp: E.matmul(
                                    p6[:], VG[:, c * 128:(c + 1) * 128], eb,
                                    start=st, stop=sp), inc="sT")
                        t_pv = ent_free[(nk - 1) % 3]
                        r.wait("scalar", "sT", t_pv)
                        s_l = r.do("scalar", lambda E, S: E.activation(
                            out=ST[0:1, 0:512], in_=p4[0:1, :], func=AF.Ln), inc="sS")
                        bank_free[4] = ("sS", s_l)
                        bank_wait(7)
                        r.wait("tensor", "sS", s_l)
                        t_b = r.do("tensor", lambda E, S: E.matmul(
                            p7[:], ONESF[:], ST[0:1, 0:512], start=True, stop=True), inc="sT")
                        r.wait("scalar", "sT", t_b)
                        s_r = r.do("scalar", lambda E, S, s=s: E.activation(
                            out=RB[:, s * 512:(s + 1) * 512], in_=p7[:], func=AF.Exp,
                            scale=-1.0), inc="sS")
                        bank_free[7] = ("sS", s_r)
                        r.wait("vector", "sS", s_r)
                        c0 = h * T + s * TH
                        ao_last = r.do("vector", lambda E, S, c0=c0, s=s: E.tensor_tensor(
                            out=B2[:, c0:c0 + TH], in0=p6[:], in1=RB[:, s * 512:(s + 1) * 512],
                            op=OP.mult), inc="sV")
                        bank_free[6] = ("sV", ao_last)
                    qh_user[hs] = r.n["sT"]
                kv_consumed = r.n["sT"]
            fm_proj(woB, NC, ("sV", ao_last), sink_resid())
            return r.n["sV"]

        v = sub1()
        if debug_tap == 1:
            r.wait("gpsimd", "sV", v)
            dma("gpsimd", "sW", dbg.rearrange("(c d) t -> d c t", d=128),
                X[:].rearrange("d (c t) -> d c t", t=T))
        if n_sub >= 2:
            v = sub_ffn(v)
            if debug_tap == 2:
                r.wait("gpsimd", "sV", v)
                dma("gpsimd", "sW", dbg.rearrange("(c d) t -> d c t", d=128),
                    X[:].rearrange("d (c t) -> d c t", t=T))
        if n_sub >= 3:
            v = sub3(v)
            if debug_tap == 3:
                r.wait("gpsimd", "sV", v)
                dma("gpsimd", "sW", dbg.rearrange("(c d) t -> d c t", d=128),
                    X[:].rearrange("d (c t) -> d c t", t=T))
        if n_sub >= 4:
            v = sub_ffn(v)

        r.wait("gpsimd", "sV", v)
        fin = dma("gpsimd", "sW", out.rearrange("(c d) t -> d c t", d=128),
                  X[:].rearrange("d (c t) -> d c t", t=T))
        r.wait("gpsimd", "sW", fin)

        @block.sync
        def _(E):
            for f in r.ops["sync"]:
                f(E, S)

        @block.gpsimd
        def _(E):
            for f in r.ops["gpsimd"]:
                f(E, S)

        @block.scalar
        def _(E):
            for f in r.ops["scalar"]:
                f(E, S)

        @block.vector
        def _(E):
            for f in r.ops["vector"]:
                f(E, S)

        @block.tensor
        def _(E):
            for f in r.ops["tensor"]:
                f(E, S)

    return nc



def _prep_shared(i):
    f32 = np.float32
    sh = {}
    sh["wqT"] = np.ascontiguousarray(i["L0_attn_a_Wq"].T.astype(f32))
    sh["wkT"] = np.ascontiguousarray(i["L0_attn_a_Wk"].T.astype(f32))
    sh["wvT"] = np.ascontiguousarray(i["L0_attn_a_Wv"].T.astype(f32))
    sh["woT"] = np.ascontiguousarray(i["L0_attn_a_Wo"].T.astype(f32))
    sh["bq_t"] = np.ascontiguousarray(i["L0_attn_a_bq"].reshape(32, 128).astype(f32))
    sh["bk_t"] = np.ascontiguousarray(i["L0_attn_a_bk"].reshape(32, 128).astype(f32))
    sh["bv_d"] = np.ascontiguousarray(i["L0_attn_a_bv"].astype(f32))
    sh["bo_d"] = np.ascontiguousarray(i["L0_attn_a_bo"].astype(f32))
    sh["bo_t"] = np.ascontiguousarray(i["L0_attn_a_bo"].reshape(32, 128).astype(f32))
    sh["bbase_t"] = np.ascontiguousarray(i["L0_ffn_b_base"].reshape(32, 128).astype(f32))
    wqkv = i["L0_attn_b_w_qkv"].astype(f32)
    sh["wQb"] = np.ascontiguousarray(wqkv[:, 0:4096][:, _pair_perm(32)])
    sh["wKb"] = np.ascontiguousarray(wqkv[:, 4096:5120][:, _pair_perm(8)])
    sh["wVb"] = np.ascontiguousarray(wqkv[:, 5120:6144])
    sh["woB"] = np.ascontiguousarray(i["L0_attn_b_w_o"].astype(f32))
    sh["wbaseT"] = np.ascontiguousarray(i["L0_ffn_W_base"].T.astype(f32))
    sh["bbase_d"] = np.ascontiguousarray(i["L0_ffn_b_base"].astype(f32))
    sh["wrouT"] = np.ascontiguousarray(i["L0_ffn_W_router"].T.astype(f32))
    sh["brou"] = np.ascontiguousarray(i["L0_ffn_b_router"].reshape(8, 1).astype(f32))
    sh["a_all"] = np.ascontiguousarray(
        i["L0_ffn_lora_A"].transpose(1, 0, 2).reshape(4096, 128).astype(f32))
    sh["b_all"] = np.ascontiguousarray(i["L0_ffn_lora_B"].reshape(128, 4096).astype(f32))
    sh["maskil_d"] = np.ascontiguousarray(
        ((np.arange(128)[:, None] % 4) == (np.arange(128)[None, :] % 4)).astype(BF))
    sh["id32_d"] = np.eye(128, dtype=f32)
    sh["ones_d"] = np.ones((128, 512), BF)
    sh["onesf_d"] = np.ones((1, 128), f32)
    pr = np.zeros((8, 128), np.float32)
    for e in range(8):
        pr[e, e * 16:(e + 1) * 16] = 1.0
    sh["prep_d"] = np.ascontiguousarray(pr.astype(BF))
    return sh


def kernel(**inputs):
    x = np.asarray(inputs["L0_attn_a_x"], np.float32)
    sh = _prep_shared(inputs)
    nc = build_program(n_sub=N_SUB, debug_tap=DEBUG_TAP)
    in_maps = []
    for c in range(8):
        b = c // 2
        pos = _core_positions(c)
        cos2, sin2 = _cos_sin(pos)
        m = dict(sh)
        m["x_in"] = np.ascontiguousarray(x[b][pos].T)
        m["cos_d"] = cos2
        m["sin_d"] = sin2
        m["cmask_d"] = _causal_masks(c)
        in_maps.append(m)
    res = run_bass_kernel_spmd(nc, in_maps, core_ids=list(range(8)))
    outp = np.empty((4, 2048, 4096), np.float32)
    for c in range(8):
        b = c // 2
        pos = _core_positions(c)
        outp[b][pos] = res.results[c]["out"].T
    if DEBUG_TAP is not None:
        kernel._dbg = [res.results[c]["dbg"].T for c in range(8)]
    return outp


# version 03
# speedup vs baseline: 1.2301x; 1.1798x over previous
import numpy as np
import ml_dtypes
import concourse.bass as bass
import concourse.mybir as mybir
from concourse.alu_op_type import AluOpType as OP
from concourse.bass_utils import run_bass_kernel_spmd

F32 = mybir.dt.float32
BF16 = mybir.dt.bfloat16
AF = mybir.ActivationFunctionType
BF = ml_dtypes.bfloat16

D = 4096
T = 1024
NC = 32
TH = 512
HD = 128
SCALE = float(np.float32(1.0) / np.float32(np.sqrt(np.float32(HD))))
EXPC = float(np.exp(np.float32(0.001)))
BSHIFT = -24.0
LNEPS = 1e-5

N_SUB = 4
DEBUG_TAP = None

SEMS = ("sW", "sD", "sD2", "sT", "sV", "sS", "sC", "sA", "sB",
        "sM1", "sM2", "sM3", "sM4", "sE", "sF")



def _core_positions(c):
    if c % 2 == 0:
        return np.concatenate([np.arange(0, 512), np.arange(1536, 2048)])
    return np.arange(512, 1536)


def _pair_perm(n_heads):
    perm = []
    for p in range(n_heads // 2):
        for h in (2 * p, 2 * p + 1):
            perm.extend((h * 128 + np.arange(0, 128, 2)).tolist())
        for h in (2 * p, 2 * p + 1):
            perm.extend((h * 128 + np.arange(1, 128, 2)).tolist())
    return np.array(perm)


def _cos_sin(pos):
    inv = (1.0 / 10000.0 ** (np.arange(0, HD, 2, dtype=np.float32) / HD)).astype(np.float32)
    ang = (pos.astype(np.float32)[:, None] * inv[None, :]).astype(np.float32)
    cos = np.cos(ang).astype(np.float32).T
    sin = np.sin(ang).astype(np.float32).T
    return (np.ascontiguousarray(np.concatenate([cos, cos], 0)),
            np.ascontiguousarray(np.concatenate([sin, sin], 0)))


def _causal_masks(c):
    qpos = [0, 1536] if c % 2 == 0 else [512, 1024]
    tiles = np.zeros((16, 128, 512), np.float32)
    kk = np.arange(128)[:, None]
    qq = np.arange(512)[None, :]
    i = 0
    for s in (0, 1):
        for ch in (range(0, 8) if s == 0 else range(8, 16)):
            tiles[i] = ((128 * ch + kk) <= (qpos[s] + qq)).astype(np.float32)
            i += 1
    return np.ascontiguousarray(tiles.reshape(16 * 128, 512).astype(BF))



class R:
    def __init__(self):
        self.ops = {e: [] for e in ("sync", "gpsimd", "scalar", "vector", "tensor")}
        self.n = {s: 0 for s in SEMS}

    def wait(self, eng, sem, val):
        if val is None or val <= 0:
            return
        def f(E, S, sem=sem, val=val):
            E.wait_ge(S[sem], val)
        self.ops[eng].append(f)

    def do(self, eng, build, inc=None, k=1):
        tgt = None
        if inc is not None:
            self.n[inc] += k
            tgt = self.n[inc]
        def f(E, S, build=build, inc=inc, k=k):
            ins = build(E, S)
            if inc is not None:
                ins.then_inc(S[inc], k)
        self.ops[eng].append(f)
        return tgt



def build_program(n_sub=4, debug_tap=None):
    nc = bass.Bass("TRN2", target_bir_lowering=False)
    dt = lambda n, s, ty, k="ExternalInput": nc.dram_tensor(n, s, ty, kind=k).ap()

    x_in = dt("x_in", [D, T], F32)
    wqT = dt("wqT", [D, D], F32)
    wkT = dt("wkT", [D, D], F32)
    wvT = dt("wvT", [D, D], F32)
    woT = dt("woT", [D, D], F32)
    bq_t = dt("bq_t", [32, 128], F32)
    bk_t = dt("bk_t", [32, 128], F32)
    bo_t = dt("bo_t", [32, 128], F32)
    bbase_t = dt("bbase_t", [32, 128], F32)
    bv_d = dt("bv_d", [D], F32)
    bo_d = dt("bo_d", [D], F32)
    wQb = dt("wQb", [D, 4096], F32)
    wKb = dt("wKb", [D, 1024], F32)
    wVb = dt("wVb", [D, 1024], F32)
    woB = dt("woB", [D, D], F32)
    wbaseT = dt("wbaseT", [D, D], F32)
    bbase_d = dt("bbase_d", [D], F32)
    wrouT = dt("wrouT", [D, 8], F32)
    brou = dt("brou", [8, 1], F32)
    a_all = dt("a_all", [D, 128], F32)
    b_all = dt("b_all", [128, D], F32)
    cos_d = dt("cos_d", [128, T], F32)
    sin_d = dt("sin_d", [128, T], F32)
    cmask_d = dt("cmask_d", [16 * 128, 512], BF16)
    maskil_d = dt("maskil_d", [128, 128], BF16)
    id32_d = dt("id32_d", [128, 128], F32)
    ones_d = dt("ones_d", [128, 512], BF16)
    onesf_d = dt("onesf_d", [1, 128], F32)
    prep_d = dt("prep_d", [8, 128], BF16)

    out = dt("out", [D, T], F32, "ExternalOutput")
    dbg = dt("dbg", [D, T], F32, "ExternalOutput") if debug_tap is not None else None

    QA_d = dt("QA_d", [D, T], BF16, "Internal")
    KA_d = dt("KA_d", [D, T], BF16, "Internal")
    VA_d = dt("VA_d", [T * 32, 128], BF16, "Internal")
    QB_d = dt("QB_d", [D, T], BF16, "Internal")
    agin_t = nc.dram_tensor("agin", [2048, 1024], BF16, kind="Internal")
    agout_t = nc.dram_tensor("agout", [4096, 1024], BF16, kind="Internal")
    agin = agin_t.ap()
    agout = agout_t.ap()

    r = R()

    from contextlib import ExitStack
    with ExitStack() as _es:
        en = _es.enter_context
        X = en(nc.sbuf_tensor("X", [128, NC * T], BF16))
        B2 = en(nc.sbuf_tensor("B2", [128, NC * T], BF16))
        SCR = en(nc.sbuf_tensor("SCR", [128, 16384], BF16))
        WP = en(nc.sbuf_tensor("WP", [128, 8192], BF16))
        SP = en(nc.sbuf_tensor("SP", [128, 1024], BF16))
        AB = en(nc.sbuf_tensor("AB", [128, 2048], F32))
        RB = en(nc.sbuf_tensor("RB", [128, 1024], F32))
        ST = en(nc.sbuf_tensor("ST", [128, 1024], F32))
        SMX = en(nc.sbuf_tensor("SMX", [128, 384], F32))
        CB = en(nc.sbuf_tensor("CB", [128, 192], F32))
        COS2 = en(nc.sbuf_tensor("COS2", [128, T], F32))
        SIN2 = en(nc.sbuf_tensor("SIN2", [128, T], F32))
        MIL = en(nc.sbuf_tensor("MIL", [128, 128], BF16))
        ID32 = en(nc.sbuf_tensor("ID32", [128, 128], F32))
        ONES = en(nc.sbuf_tensor("ONES", [128, 512], BF16))
        ONESF = en(nc.sbuf_tensor("ONESF", [1, 128], F32))
        PREP = en(nc.sbuf_tensor("PREP", [8, 128], BF16))
        p0 = en(nc.psum_tensor("p0", [128, 512], F32))
        p1 = en(nc.psum_tensor("p1", [128, 512], F32))
        p2 = en(nc.psum_tensor("p2", [128, 512], F32))
        p3 = en(nc.psum_tensor("p3", [128, 512], F32))
        p4 = en(nc.psum_tensor("p4", [128, 512], F32))
        p5 = en(nc.psum_tensor("p5", [128, 512], F32))
        p6 = en(nc.psum_tensor("p6", [128, 512], F32))
        p7 = en(nc.psum_tensor("p7", [128, 512], F32))
        hW = en(nc.semaphore("sW"))
        hD = en(nc.semaphore("sD"))
        hD2 = en(nc.semaphore("sD2"))
        hT = en(nc.semaphore("sT"))
        hV = en(nc.semaphore("sV"))
        hS = en(nc.semaphore("sS"))
        hC = en(nc.semaphore("sC"))
        hA = en(nc.semaphore("sA"))
        hB = en(nc.semaphore("sB"))
        hM1 = en(nc.semaphore("sM1"))
        hM2 = en(nc.semaphore("sM2"))
        hM3 = en(nc.semaphore("sM3"))
        hM4 = en(nc.semaphore("sM4"))
        hE = en(nc.semaphore("sE"))
        hF = en(nc.semaphore("sF"))
        block = en(nc.Block())
        S = {"sW": hW, "sD": hD, "sD2": hD2, "sT": hT, "sV": hV, "sS": hS, "sC": hC,
             "sA": hA, "sB": hB, "sM1": hM1, "sM2": hM2, "sM3": hM3, "sM4": hM4,
             "sE": hE, "sF": hF}
        P = [p0, p1, p2, p3, p4, p5, p6, p7]
        STbf = ST[:].bitcast(BF16)
        ABbf = AB[:].bitcast(BF16)

        bank_free = [None] * 8

        sp_state = {"last": ("sD", 0)}

        def bank_wait(i, eng="tensor"):
            if bank_free[i] is not None:
                sem, val = bank_free[i]
                r.wait(eng, sem, val)
                bank_free[i] = None

        def dma(eng, sem, outap, inap):
            return r.do(eng, lambda E, S, o=outap, i=inap: E.dma_start(out=o, in_=i),
                        inc=sem, k=16)

        dma("sync", "sD", COS2[:], cos_d)
        dma("sync", "sD", SIN2[:], sin_d)
        dma("sync", "sD", MIL[:], maskil_d)
        dma("sync", "sD", ID32[:], id32_d)
        dma("sync", "sD", ONES[:], ones_d)
        dma("sync", "sD", ONESF[:], onesf_d)
        dma("sync", "sD", PREP[:], prep_d)
        dma("sync", "sD", CB[0:8, 128:129], brou)
        dma("sync", "sD", RB[0:32, 0:128], bq_t)
        dma("sync", "sD", RB[0:32, 128:256], bk_t)
        dma("sync", "sD", RB[0:32, 256:384], bo_t)
        v_setup = dma("sync", "sD", RB[0:32, 384:512], bbase_t)

        dma("gpsimd", "sW", X[:].rearrange("d (c t) -> d c t", t=T),
            x_in.rearrange("(c d) t -> d c t", d=128))
        w_xload = r.n["sW"]

        r.wait("tensor", "sD", v_setup)
        r.do("tensor", lambda E, S: E.matmul(p7[:, 0:32], RB[0:32, 0:128],
                                             ID32[0:32, 0:32], start=True, stop=True))
        r.do("tensor", lambda E, S: E.matmul(p7[:, 32:64], RB[0:32, 128:256],
                                             ID32[0:32, 0:32], start=True, stop=True))
        r.do("tensor", lambda E, S: E.matmul(p7[:, 64:96], RB[0:32, 256:384],
                                             ID32[0:32, 0:32], start=True, stop=True))
        t_bias = r.do("tensor", lambda E, S: E.matmul(p7[:, 96:128], RB[0:32, 384:512],
                                                      ID32[0:32, 0:32], start=True, stop=True),
                      inc="sT")
        r.wait("vector", "sT", t_bias)
        v_bias = r.do("vector", lambda E, S: E.tensor_copy(CB[:, 0:128], p7[:, 0:128]),
                      inc="sV")
        bank_free[7] = ("sV", v_bias)

        def emit_ln(xr_sem, xr_val):
            XSQ = [SCR[:, 0:1024], SCR[:, 1024:2048]]
            r.wait("scalar", xr_sem, xr_val)
            r.wait("tensor", xr_sem, xr_val)
            r.do("vector", lambda E, S: E.drain())
            for i in range(4):
                bank_wait(i)
            sq_done = [None] * NC
            mm_done = [None] * NC
            for j in range(NC):
                if j >= 2:
                    r.wait("scalar", "sT", mm_done[j - 2])
                sq_done[j] = r.do("scalar", lambda E, S, j=j: E.activation(
                    out=XSQ[j % 2], in_=X[:, j * T:(j + 1) * T], func=AF.Square), inc="sS")
                r.wait("tensor", "sS", sq_done[j])
                st, sp = (j == 0), (j == NC - 1)
                r.do("tensor", lambda E, S, j=j, st=st, sp=sp: E.matmul(
                    p0[:], ONES[:, 0:128], X[:, j * T:j * T + TH], start=st, stop=sp))
                r.do("tensor", lambda E, S, j=j, st=st, sp=sp: E.matmul(
                    p1[:], ONES[:, 0:128], X[:, j * T + TH:(j + 1) * T], start=st, stop=sp))
                r.do("tensor", lambda E, S, j=j, st=st, sp=sp: E.matmul(
                    p2[:], ONES[:, 0:128], XSQ[j % 2][:, 0:TH], start=st, stop=sp))
                mm_done[j] = r.do("tensor", lambda E, S, j=j, st=st, sp=sp: E.matmul(
                    p3[:], ONES[:, 0:128], XSQ[j % 2][:, TH:T], start=st, stop=sp), inc="sT")
            r.wait("scalar", "sT", mm_done[NC - 1])
            inv = 1.0 / D
            prev_t1 = prev_v2 = None
            for h in range(2):
                psx, psxx = (p0, p2) if h == 0 else (p1, p3)
                if h == 1:
                    r.wait("scalar", "sT", prev_t1)
                    r.wait("scalar", "sV", prev_v2)
                SCRf = SCR[:].bitcast(F32)
                sm = SCRf[0:1, 1024:1536]
                sm2 = SCRf[0:1, 1536:2048]
                sx3 = SCRf[0:1, 2048:2560]
                r.do("scalar", lambda E, S, psx=psx, sm=sm: E.activation(
                    out=sm, in_=psx[0:1, :], func=AF.Copy, scale=inv))
                r.do("scalar", lambda E, S, psx=psx, sm2=sm2: E.activation(
                    out=sm2, in_=psx[0:1, :], func=AF.Square, scale=inv))
                s1 = r.do("scalar", lambda E, S, psxx=psxx, sx3=sx3: E.activation(
                    out=sx3, in_=psxx[0:1, :], func=AF.Copy, scale=inv), inc="sS")
                r.wait("vector", "sS", s1)
                v1 = r.do("vector", lambda E, S, sm2=sm2, sx3=sx3: E.scalar_tensor_tensor(
                    out=sm2, in0=sx3, scalar=LNEPS, in1=sm2,
                    op0=OP.add, op1=OP.subtract), inc="sV")
                r.wait("scalar", "sV", v1)
                r.do("scalar", lambda E, S, sm2=sm2, sx3=sx3: E.activation(
                    out=sx3, in_=sm2, func=AF.Ln))
                r.do("scalar", lambda E, S: E.drain())
                s2 = r.do("scalar", lambda E, S, sx3=sx3: E.activation(
                    out=ST[0:1, 0:512], in_=sx3, func=AF.Exp, scale=-0.5),
                    inc="sS")
                r.wait("vector", "sS", s2)
                v2 = r.do("vector", lambda E, S, sm=sm: E.scalar_tensor_tensor(
                    out=ST[0:1, 512:1024], in0=sm, scalar=-1.0,
                    in1=ST[0:1, 0:512], op0=OP.mult, op1=OP.mult), inc="sV")
                bank_wait(4)
                bank_wait(5)
                r.wait("tensor", "sV", v2)
                r.do("tensor", lambda E, S: E.matmul(p4[:], ONESF[:], ST[0:1, 0:512],
                                                     start=True, stop=True))
                t1 = r.do("tensor", lambda E, S: E.matmul(p5[:], ONESF[:],
                                                          ST[0:1, 512:1024],
                                                          start=True, stop=True), inc="sT")
                r.wait("vector", "sT", t1)
                r.do("vector", lambda E, S, h=h: E.tensor_copy(AB[:, h * TH:(h + 1) * TH],
                                                               p4[:]))
                v3 = r.do("vector", lambda E, S, h=h: E.tensor_copy(
                    AB[:, 1024 + h * TH:1024 + (h + 1) * TH], p5[:]), inc="sV")
                bank_free[4] = ("sV", v3)
                bank_free[5] = ("sV", v3)
                prev_t1, prev_v2 = t1, v2
            for i, v in ((0, None), (1, None), (2, None), (3, None)):
                bank_free[i] = ("sS", r.n["sS"])
            r.do("vector", lambda E, S: E.drain())
            last = None
            for j in range(NC):
                for h in range(2):
                    c0 = j * T + h * TH
                    r.do("vector", lambda E, S, c0=c0, h=h: E.tensor_tensor(
                        out=RB[:, h * TH:(h + 1) * TH], in0=X[:, c0:c0 + TH],
                        in1=AB[:, h * TH:(h + 1) * TH], op=OP.mult))
                r.do("vector", lambda E, S: E.drain())
                for h in range(2):
                    c0 = j * T + h * TH
                    last = r.do("vector", lambda E, S, c0=c0, h=h: E.tensor_tensor(
                        out=B2[:, c0:c0 + TH], in0=RB[:, h * TH:(h + 1) * TH],
                        in1=AB[:, 1024 + h * TH:1024 + (h + 1) * TH], op=OP.add), inc="sV")
                r.do("vector", lambda E, S: E.drain())
            return last

        def fm_proj(Wd, n_o, rhs_ready, sink, bias_row=None, extra=None,
                    banks=(0, 1, 2, 3)):
            panel_consumed = [None] * n_o
            for o in range(n_o):
                wp = WP[:, (o % 2) * 4096:(o % 2) * 4096 + 4096]
                psem = "sA" if o % 2 == 0 else "sB"
                if o >= 2:
                    r.wait("gpsimd", "sT", panel_consumed[o - 2])
                else:
                    r.wait("gpsimd", "sT", r.n["sT"])
                wa = dma("gpsimd", psem, wp.rearrange("d (k o) -> d k o", o=128),
                         Wd.rearrange("(k d) o -> d k o", d=128)[:, :, o * 128:(o + 1) * 128])
                r.wait("tensor", psem, wa)
                if o == 0:
                    r.wait("tensor", rhs_ready[0], rhs_ready[1])
                for tc in range(2):
                    bi = banks[(2 * o + tc) % len(banks)]
                    bank_wait(bi)
                    for k in range(NC):
                        st = (k == 0)
                        last = (k == NC - 1) and bias_row is None and extra is None
                        tv0 = r.do("tensor",
                                   lambda E, S, bi=bi, k=k, tc=tc, wp=wp, st=st, sp=last:
                                   E.matmul(P[bi][:], wp[:, k * 128:(k + 1) * 128],
                                            B2[:, k * T + tc * TH:k * T + tc * TH + TH],
                                            start=st, stop=sp),
                                   inc="sT" if last else None)
                        if last:
                            tv = tv0
                    if bias_row is not None:
                        last = extra is None
                        tv0 = r.do("tensor",
                                   lambda E, S, bi=bi, o=o, br=bias_row, sp=last:
                                   E.matmul(P[bi][:], br[0:1, o * 128:(o + 1) * 128],
                                            ONES[0:1, 0:TH], start=False, stop=sp),
                                   inc="sT" if last else None)
                        if last:
                            tv = tv0
                    if extra is not None:
                        tv = extra(o, tc, bi)
                    bank_free[bi] = sink(o, tc, bi, tv)
                panel_consumed[o] = r.n["sT"]

        def sink_spill(dram, bias_col=None):
            spill_done = {}
            def sink(o, tc, bi, tv):
                sb = SP[:, tc * TH:(tc + 1) * TH]
                if o == 0 and tc == 0:
                    r.wait("scalar", sp_state["last"][0], sp_state["last"][1])
                if o >= 1 and (o - 1) in spill_done and tc == 0:
                    r.wait("scalar", "sD", spill_done[o - 1])
                r.wait("scalar", "sT", tv)
                if bias_col is not None:
                    sv = r.do("scalar", lambda E, S, sb=sb, bi=bi, o=o: E.activation(
                        out=sb, in_=P[bi][:], func=AF.Identity,
                        bias=CB[:, bias_col + o:bias_col + o + 1]), inc="sS")
                else:
                    sv = r.do("scalar", lambda E, S, sb=sb, bi=bi: E.activation(
                        out=sb, in_=P[bi][:], func=AF.Copy), inc="sS")
                if tc == 1:
                    r.wait("sync", "sS", sv)
                    spill_done[o] = dma("sync", "sD", dram[o * 128:(o + 1) * 128, :],
                                        SP[:, 0:1024])
                    sp_state["last"] = ("sD", spill_done[o])
                return ("sS", sv)
            sink.spill_done = spill_done
            return sink

        def sink_resid(bias_col=None):
            def sink(o, tc, bi, tv):
                c0 = o * T + tc * TH
                if bias_col is None:
                    r.wait("vector", "sT", tv)
                    vv = r.do("vector", lambda E, S, c0=c0, bi=bi: E.tensor_tensor(
                        out=X[:, c0:c0 + TH], in0=P[bi][:], in1=X[:, c0:c0 + TH],
                        op=OP.add), inc="sV")
                    return ("sV", vv)
                slot = (2 * o + tc) % 2
                tmp = SCR[:, slot * 512:(slot + 1) * 512]
                if slot in sink.tmp_user:
                    r.wait("scalar", "sV", sink.tmp_user[slot])
                r.wait("scalar", "sT", tv)
                sv = r.do("scalar", lambda E, S, tmp=tmp, bi=bi, o=o: E.activation(
                    out=tmp, in_=P[bi][:], func=AF.Identity,
                    bias=CB[:, bias_col + o:bias_col + o + 1]), inc="sS")
                r.wait("vector", "sS", sv)
                vv = r.do("vector", lambda E, S, c0=c0, tmp=tmp: E.tensor_tensor(
                    out=X[:, c0:c0 + TH], in0=tmp, in1=X[:, c0:c0 + TH],
                    op=OP.add), inc="sV")
                sink.tmp_user[slot] = vv
                return ("sV", vv)
            sink.tmp_user = {}
            return sink

        def load_bias_row(dst_row, dd, n, after_sem, after_val, sem="sM1"):
            r.wait("gpsimd", after_sem, after_val)
            return dma("gpsimd", sem, dst_row[0:1, 0:n],
                       dd.rearrange("(a b) -> a b", a=1))

        def tm_proj(Wd, rhs_ready, dst_fn, n_oh=2, bias_bf_row=None):
            r.wait("tensor", rhs_ready[0], rhs_ready[1])
            tiles = [SCR[:, 2048:2560], SCR[:, 2560:3072]]
            half_done = {}
            prev_used = None
            for oh in range(n_oh):
                tile_used = [None] * NC
                for k in range(NC):
                    tsem = "sA" if k % 2 == 0 else "sB"
                    if k >= 2:
                        r.wait("gpsimd", "sT", tile_used[k - 2])
                    elif prev_used is not None:
                        r.wait("gpsimd", "sT", prev_used[NC - 2 + k])
                    else:
                        r.wait("gpsimd", "sT", r.n["sT"])
                    wa = dma("gpsimd", tsem, tiles[k % 2],
                             Wd.rearrange("(k d) o -> d k o", d=128)[:, k,
                                          oh * 512:(oh + 1) * 512])
                    r.wait("tensor", tsem, wa)
                    for tch in range(8):
                        if k == 0:
                            bank_wait(tch)
                        st = (k == 0)
                        sp = (k == NC - 1) and bias_bf_row is None
                        tu = r.do("tensor",
                                  lambda E, S, tch=tch, k=k, st=st, sp=sp, tl=tiles[k % 2]:
                                  E.matmul(P[tch][:],
                                           B2[:, k * T + tch * 128:k * T + (tch + 1) * 128],
                                           tl, start=st, stop=sp),
                                  inc="sT" if tch == 7 else None)
                    tile_used[k] = tu
                prev_used = tile_used
                for tch in range(8):
                    if bias_bf_row is not None:
                        tv = r.do("tensor", lambda E, S, tch=tch, oh=oh, br=bias_bf_row:
                                  E.matmul(P[tch][:], ONES[0:1, 0:128],
                                           br[0:1, oh * 512:(oh + 1) * 512],
                                           start=False, stop=True), inc="sT")
                    else:
                        tv = tile_used[NC - 1]
                    hsem = "sE" if oh % 2 == 0 else "sF"
                    if (oh % 2) in half_done:
                        r.wait("scalar", hsem, half_done[oh % 2])
                    elif oh < 2 and tch == 0:
                        r.wait("scalar", sp_state["last"][0], sp_state["last"][1])
                    r.wait("scalar", "sT", tv)
                    sv = r.do("scalar", lambda E, S, tch=tch, oh=oh: E.activation(
                        out=SP[:, (oh % 2) * TH:(oh % 2 + 1) * TH],
                        in_=P[tch][:], func=AF.Copy), inc="sS")
                    bank_free[tch] = ("sS", sv)
                    r.wait("sync", "sS", sv)
                    oap, iap = dst_fn(tch, oh, SP[:, (oh % 2) * TH:(oh % 2 + 1) * TH])
                    half_done[oh % 2] = dma("sync", hsem, oap, iap)
                    sp_state["last"] = (hsem, half_done[oh % 2])
            return (r.n["sE"], r.n["sF"])

        tap_points = {}

        def sub1():
            ln1 = emit_ln("sW", w_xload)
            skq = sink_spill(QA_d, bias_col=0)
            fm_proj(wqT, NC, ("sV", ln1), skq)
            skk = sink_spill(KA_d, bias_col=32)
            fm_proj(wkT, NC, ("sV", ln1), skk)
            bw = load_bias_row(ABbf, bv_d, D, "sV", ln1)
            r.wait("tensor", "sM1", bw)
            va_view = VA_d.rearrange("(tt g tp) d -> tt tp g d", g=32, tp=128)

            def va_dst(tch, oh, src):
                return (va_view[tch][:, 4 * oh:4 * oh + 4, :],
                        src.rearrange("p (g4 d) -> p g4 d", d=128))
            vE, vF = tm_proj(wvT, ("sV", ln1), va_dst, n_oh=8, bias_bf_row=ABbf)


            spills = max(max(skq.spill_done.values()), max(skk.spill_done.values()))

            QC, KC, VR, EN = (SCR[:, 0:4096], SCR[:, 4096:8192],
                              SCR[:, 8192:12288], SCR[:, 12288:16384])
            ao_last = None
            for tt in range(8):
                t0 = tt * 128
                r.wait("sync", "sT", r.n["sT"])
                if tt == 0:
                    r.wait("sync", "sD", spills)
                    r.wait("sync", "sE", vE)
                    r.wait("sync", "sF", vF)
                dma("sync", "sD", QC.rearrange("d (h t) -> d h t", t=128),
                    QA_d.rearrange("(h d) t -> d h t", d=128)[:, :, t0:t0 + 128])
                dma("sync", "sD", KC.rearrange("d (h t) -> d h t", t=128),
                    KA_d.rearrange("(h d) t -> d h t", d=128)[:, :, t0:t0 + 128])
                ld = dma("sync", "sD",
                         VR.rearrange("p (gp d) -> p gp d", d=128),
                         VA_d.rearrange("(tt g j gp) d -> tt (g j) gp d",
                                        tt=8, g=32, j=4)[tt])
                r.wait("tensor", "sD", ld)
                def hm_tail(gq, v_u):
                    nonlocal ao_last
                    pvbank = P[5 + (gq % 2)]
                    bank_wait(4)
                    bank_wait(5 + (gq % 2))
                    r.wait("tensor", "sV", v_u)
                    for gg in range(4):
                        g = gq * 4 + gg
                        r.do("tensor", lambda E, S, g=g, gg=gg: E.matmul(
                            p4[:, gg * 128:(gg + 1) * 128], ONES[:, 0:128],
                            EN[:, g * 128:(g + 1) * 128], start=True, stop=True))
                        t_d = r.do("tensor", lambda E, S, g=g, gg=gg, pvbank=pvbank: E.matmul(
                            pvbank[:, gg * 128:(gg + 1) * 128], VR[:, g * 128:(g + 1) * 128],
                            EN[:, g * 128:(g + 1) * 128], start=True, stop=True),
                            inc="sT" if gg == 3 else None)
                    r.wait("scalar", "sT", t_d)
                    s_l = r.do("scalar", lambda E, S: E.activation(
                        out=ST[0:1, 0:512], in_=p4[0:1, :], func=AF.Ln), inc="sS")
                    bank_free[4] = ("sS", s_l)
                    bank_wait(7)
                    r.wait("tensor", "sS", s_l)
                    t_b = r.do("tensor", lambda E, S: E.matmul(
                        p7[:], ONESF[:], ST[0:1, 0:512], start=True, stop=True), inc="sT")
                    r.wait("scalar", "sT", t_b)
                    s_r = r.do("scalar", lambda E, S, gq=gq: E.activation(
                        out=RB[:, (gq % 2) * 512:(gq % 2) * 512 + 512], in_=p7[:],
                        func=AF.Exp, scale=-1.0), inc="sS")
                    bank_free[7] = ("sS", s_r)
                    r.wait("vector", "sS", s_r)
                    dst = B2[:].rearrange("p (h jj aa) -> p aa h jj", h=32, jj=32, aa=32)[
                        :, 4 * gq:4 * gq + 4, :, 4 * tt:4 * tt + 4]
                    srcv = pvbank[:].rearrange("p (a h j) -> p a h j", a=4, h=32)
                    rrv = RB[:, (gq % 2) * 512:(gq % 2) * 512 + 512].rearrange(
                        "p (a h j) -> p a h j", a=4, h=32)
                    ao_last = r.do("vector", lambda E, S, dst=dst, srcv=srcv, rrv=rrv:
                                   E.tensor_tensor(out=dst, in0=srcv, in1=rrv, op=OP.mult),
                                   inc="sV")
                    bank_free[5 + (gq % 2)] = ("sV", ao_last)

                pend = None
                for gq in range(8):
                    sbank = P[gq % 2]
                    bank_wait(gq % 2)
                    for gg in range(4):
                        g = gq * 4 + gg
                        t_s = r.do("tensor", lambda E, S, g=g, gg=gg, sbank=sbank: E.matmul(
                            sbank[:, gg * 128:(gg + 1) * 128], KC[:, g:4096:32],
                            QC[:, g:4096:32], start=True, stop=True),
                            inc="sT" if gg == 3 else None)
                    r.wait("scalar", "sT", t_s)
                    s_e = r.do("scalar", lambda E, S, gq=gq, sbank=sbank: E.activation(
                        out=EN[:, gq * 512:(gq + 1) * 512], in_=sbank[:], func=AF.Exp,
                        scale=SCALE), inc="sS")
                    r.wait("vector", "sS", s_e)
                    r.do("vector", lambda E, S, gq=gq, sbank=sbank: E.tensor_tensor(
                        out=sbank[:].rearrange("p (a b) -> p a b", b=128),
                        in0=MIL[:].unsqueeze(1).broadcast_to([128, 4, 128]),
                        in1=EN[:, gq * 512:(gq + 1) * 512].rearrange("p (a b) -> p a b",
                                                                     b=128),
                        op=OP.mult))
                    r.do("vector", lambda E, S: E.drain())
                    v_u = r.do("vector", lambda E, S, gq=gq, sbank=sbank:
                               E.scalar_tensor_tensor(
                                   out=EN[:, gq * 512:(gq + 1) * 512], in0=sbank[:],
                                   scalar=EXPC, in1=EN[:, gq * 512:(gq + 1) * 512],
                                   op0=OP.is_gt, op1=OP.mult), inc="sV")
                    bank_free[gq % 2] = ("sV", v_u)
                    if pend is not None:
                        hm_tail(*pend)
                    pend = (gq, v_u)
                hm_tail(*pend)
            fm_proj(woT, NC, ("sV", ao_last), sink_resid(bias_col=64))
            return r.n["sV"]

        def sub_ffn(prev_v):
            ln = emit_ln("sV", prev_v)
            ROUT = SCR[:, 0:256]
            r.wait("gpsimd", "sV", r.n["sV"])
            r.wait("gpsimd", "sT", r.n["sT"])
            rw = dma("gpsimd", "sM2", ROUT.rearrange("d (k e) -> d k e", e=8),
                     wrouT.rearrange("(k d) e -> d k e", d=128))
            r.wait("tensor", "sM2", rw)
            r.wait("tensor", "sV", ln)
            bank_wait(4)
            bank_wait(5)
            for tc in range(2):
                pb = p4 if tc == 0 else p5
                for k in range(NC):
                    st, sp = (k == 0), (k == NC - 1)
                    t_r = r.do("tensor",
                               lambda E, S, pb=pb, k=k, tc=tc, st=st, sp=sp: E.matmul(
                                   pb[0:8, :], ROUT[:, k * 8:(k + 1) * 8],
                                   B2[:, k * T + tc * TH:k * T + tc * TH + TH],
                                   start=st, stop=sp),
                               inc="sT" if (tc == 1 and k == NC - 1) else None)
            r.wait("scalar", "sT", t_r)
            for tc in range(2):
                pb = p4 if tc == 0 else p5
                s_lg = r.do("scalar", lambda E, S, pb=pb, tc=tc: E.activation(
                    out=RB[0:8, tc * TH:(tc + 1) * TH], in_=pb[0:8, :], func=AF.Identity,
                    bias=CB[0:8, 64:65]), inc="sS" if tc == 1 else None)
            bank_free[4] = ("sS", s_lg)
            bank_free[5] = ("sS", s_lg)
            bank_wait(6)
            r.wait("tensor", "sS", s_lg)
            for c in range(8):
                t_t = r.do("tensor", lambda E, S, c=c: E.matmul(
                    p6[:, c * 8:(c + 1) * 8], RB[0:8, c * 128:(c + 1) * 128],
                    ID32[0:8, 0:8], start=True, stop=True),
                    inc="sT" if c == 7 else None)
            r.wait("vector", "sT", t_t)
            v_lt = r.do("vector", lambda E, S: E.tensor_copy(SMX[:, 0:64], p6[:, 0:64]),
                        inc="sV")
            bank_free[6] = ("sV", v_lt)
            r.do("vector", lambda E, S: E.drain())

            LT = lambda e: SMX[:, e * 8:e * 8 + 8] if False else None
            cols = lambda b, e: SMX[:, b + e:b + 64:8]
            def vd(build):
                v = r.do("vector", build, inc="sV")
                r.do("vector", lambda E, S: E.drain())
                return v
            vd(lambda E, S: E.tensor_copy(SMX[:, 64:128], SMX[:, 0:64]))
            net = [(0, 1), (2, 3), (4, 5), (6, 7), (0, 2), (1, 3), (4, 6), (5, 7),
                   (1, 2), (5, 6), (0, 4), (3, 7), (1, 5), (2, 6), (1, 4), (3, 6),
                   (2, 4), (3, 5), (3, 4)]
            for (a, b) in net:
                ca, cb = cols(64, a), cols(64, b)
                tmp = SMX[:, 280:288]
                vd(lambda E, S, ca=ca, cb=cb, tmp=tmp: E.tensor_tensor(
                    out=tmp, in0=ca, in1=cb, op=OP.max))
                vd(lambda E, S, ca=ca, cb=cb: E.tensor_tensor(
                    out=cb, in0=ca, in1=cb, op=OP.min))
                vd(lambda E, S, ca=ca, tmp=tmp: E.tensor_copy(ca, tmp))
            vd(lambda E, S: E.tensor_copy(cols(128, 0), cols(64, 0)))
            for e in range(1, 8):
                vd(lambda E, S, e=e: E.tensor_tensor(
                    out=cols(128, e), in0=cols(128, e - 1), in1=cols(64, e), op=OP.add))
            for e in range(8):
                vd(lambda E, S, e=e: E.tensor_scalar(
                    out=SMX[:, 280:288], in0=cols(64, e), scalar1=float(e + 1),
                    scalar2=1.0, op0=OP.mult, op1=OP.add))
                vd(lambda E, S, e=e: E.tensor_tensor(
                    out=cols(192, e), in0=SMX[:, 280:288], in1=cols(128, e), op=OP.is_gt))
            vd(lambda E, S: E.tensor_tensor(
                out=SMX[:, 256:264], in0=cols(192, 0), in1=cols(192, 1), op=OP.add))
            for e in range(2, 8):
                vd(lambda E, S, e=e: E.tensor_tensor(
                    out=SMX[:, 256:264], in0=SMX[:, 256:264], in1=cols(192, e), op=OP.add))
            vd(lambda E, S: E.tensor_tensor(
                out=SMX[:, 264:272], in0=cols(192, 0), in1=cols(64, 0), op=OP.mult))
            for e in range(1, 8):
                vd(lambda E, S, e=e: E.tensor_tensor(
                    out=SMX[:, 280:288], in0=cols(192, e), in1=cols(64, e), op=OP.mult))
                vd(lambda E, S: E.tensor_tensor(
                    out=SMX[:, 264:272], in0=SMX[:, 264:272], in1=SMX[:, 280:288],
                    op=OP.add))
            v_kz = r.n["sV"]
            r.wait("scalar", "sV", v_kz)
            r.do("scalar", lambda E, S: E.activation(out=SMX[:, 280:288],
                                                     in_=SMX[:, 256:264], func=AF.Ln))
            r.do("scalar", lambda E, S: E.drain())
            s_rk = r.do("scalar", lambda E, S: E.activation(
                out=SMX[:, 256:264], in_=SMX[:, 280:288], func=AF.Exp, scale=-1.0),
                inc="sS")
            r.wait("vector", "sS", s_rk)
            vd(lambda E, S: E.scalar_tensor_tensor(
                out=SMX[:, 272:280], in0=SMX[:, 264:272], scalar=-1.0, in1=SMX[:, 256:264],
                op0=OP.add, op1=OP.mult))
            v_w = None
            for e in range(8):
                vd(lambda E, S, e=e: E.tensor_tensor(
                    out=SMX[:, 280:288], in0=cols(0, e), in1=SMX[:, 272:280],
                    op=OP.subtract))
                v_w = vd(lambda E, S, e=e: E.tensor_single_scalar(
                    out=cols(288, e), in_=SMX[:, 280:288], scalar=0.0, op=OP.max))

            WFM = SCR[0:8, 256:1280]
            bank_wait(4)
            bank_wait(5)
            r.wait("tensor", "sV", v_w)
            for c in range(8):
                pb = p4 if c < 4 else p5
                t_w = r.do("tensor", lambda E, S, c=c, pb=pb: E.matmul(
                    pb[0:8, (c % 4) * 128:(c % 4 + 1) * 128],
                    SMX[:, 288 + c * 8:288 + (c + 1) * 8], ID32[:], start=True, stop=True),
                    inc="sT" if c == 7 else None)
            r.wait("scalar", "sT", t_w)
            r.do("scalar", lambda E, S: E.activation(out=WFM[:, 0:512], in_=p4[0:8, :],
                                                     func=AF.Copy))
            s_wf = r.do("scalar", lambda E, S: E.activation(
                out=WFM[:, 512:1024], in_=p5[0:8, :], func=AF.Copy), inc="sS")
            bank_free[4] = ("sS", s_wf)
            bank_free[5] = ("sS", s_wf)
            WREP = SCR[:, 1280:2304]
            bank_wait(6)
            bank_wait(7)
            r.wait("tensor", "sS", s_wf)
            r.do("tensor", lambda E, S: E.matmul(p6[:], PREP[:], WFM[:, 0:512],
                                                 start=True, stop=True))
            t_wr = r.do("tensor", lambda E, S: E.matmul(p7[:], PREP[:], WFM[:, 512:1024],
                                                        start=True, stop=True), inc="sT")
            r.wait("scalar", "sT", t_wr)
            r.do("scalar", lambda E, S: E.activation(out=WREP[:, 0:512], in_=p6[:],
                                                     func=AF.Copy))
            s_wr = r.do("scalar", lambda E, S: E.activation(
                out=WREP[:, 512:1024], in_=p7[:], func=AF.Copy), inc="sS")
            bank_free[6] = ("sS", s_wr)
            bank_free[7] = ("sS", s_wr)

            APAN = SCR[:, 2304:6400]
            aw = dma("gpsimd", "sM3", APAN.rearrange("d (k r) -> d k r", r=128),
                     a_all.rearrange("(k d) r -> d k r", d=128))
            G = SCR[:, 6400:7424]
            bank_wait(2)
            bank_wait(3)
            r.wait("tensor", "sM3", aw)
            for tc in range(2):
                pb = p2 if tc == 0 else p3
                for k in range(NC):
                    st, sp = (k == 0), (k == NC - 1)
                    t_h = r.do("tensor",
                               lambda E, S, pb=pb, k=k, tc=tc, st=st, sp=sp: E.matmul(
                                   pb[:], APAN[:, k * 128:(k + 1) * 128],
                                   B2[:, k * T + tc * TH:k * T + tc * TH + TH],
                                   start=st, stop=sp),
                               inc="sT" if (tc == 1 and k == NC - 1) else None)
            r.wait("vector", "sT", t_h)
            r.wait("vector", "sS", s_wr)
            r.do("vector", lambda E, S: E.tensor_tensor(
                out=G[0:128, 0:512], in0=p2[:], in1=WREP[:, 0:512], op=OP.mult))
            v_g = r.do("vector", lambda E, S: E.tensor_tensor(
                out=G[0:128, 512:1024], in0=p3[:], in1=WREP[:, 512:1024], op=OP.mult),
                inc="sV")
            bank_free[2] = ("sV", v_g)
            bank_free[3] = ("sV", v_g)

            BPAN = SCR[:, 7424:11520]
            bw2 = dma("gpsimd", "sM4", BPAN.rearrange("p (k o) -> p k o", o=128),
                      b_all.rearrange("er (k o) -> er k o", o=128))
            r.wait("tensor", "sM4", bw2)
            r.wait("tensor", "sV", v_g)

            def extra(o, tc, bi):
                return r.do("tensor", lambda E, S, o=o, tc=tc, bi=bi: E.matmul(
                    P[bi][:], BPAN[:, o * 128:(o + 1) * 128], G[:, tc * TH:(tc + 1) * TH],
                    start=False, stop=True), inc="sT")
            fm_proj(wbaseT, NC, ("sV", v_g), sink_resid(bias_col=96), extra=extra)
            return r.n["sV"]

        def sub3(prev_v):
            ln = emit_ln("sV", prev_v)
            RPA, RPB = SCR[:, 0:1024], SCR[:, 1024:2048]
            R1S, R2S = SCR[:, 3072:4096], SCR[:, 4096:5120]

            def rope_pairs(Wd, n_o, dst_dram, dst_head0):
                rope_free = r.n["sV"]
                detang = {}
                panel_consumed = [None] * n_o
                for o in range(n_o):
                    wp = WP[:, (o % 2) * 4096:(o % 2) * 4096 + 4096]
                    psem = "sA" if o % 2 == 0 else "sB"
                    if o >= 2:
                        r.wait("gpsimd", "sT", panel_consumed[o - 2])
                    else:
                        r.wait("gpsimd", "sT", r.n["sT"])
                    wa = dma("gpsimd", psem, wp.rearrange("d (k o) -> d k o", o=128),
                             Wd.rearrange("(k d) o -> d k o", d=128)[:, :,
                                          o * 128:(o + 1) * 128])
                    r.wait("tensor", psem, wa)
                    if o == 0:
                        r.wait("tensor", "sV", ln)
                    for tc in range(2):
                        bi = (2 * o + tc) % 4
                        bank_wait(bi)
                        for k in range(NC):
                            st, sp = (k == 0), (k == NC - 1)
                            tv0 = r.do("tensor",
                                       lambda E, S, bi=bi, k=k, tc=tc, wp=wp, st=st, sp=sp:
                                       E.matmul(P[bi][:], wp[:, k * 128:(k + 1) * 128],
                                                B2[:, k * T + tc * TH:k * T + tc * TH + TH],
                                                start=st, stop=sp),
                                       inc="sT" if sp else None)
                        tv = tv0
                        buf = RPA if o % 2 == 0 else RPB
                        if rope_free is not None and tc == 0:
                            r.wait("scalar", "sV", rope_free)
                        r.wait("scalar", "sT", tv)
                        sv = r.do("scalar", lambda E, S, buf=buf, tc=tc, bi=bi: E.activation(
                            out=buf[:, tc * TH:(tc + 1) * TH], in_=P[bi][:], func=AF.Copy),
                            inc="sS")
                        bank_free[bi] = ("sS", sv)
                    panel_consumed[o] = r.n["sT"]
                    if o % 2 == 1:
                        p = o // 2
                        r.wait("vector", "sS", r.n["sS"])
                        if p >= 1:
                            r.wait("vector", "sD", detang[p - 1])
                        else:
                            r.wait("vector", "sD", r.n["sD"])
                        r.do("vector", lambda E, S: E.tensor_tensor(
                            out=AB[:, 0:1024], in0=RPA[:], in1=COS2[:], op=OP.mult))
                        r.do("vector", lambda E, S: E.tensor_tensor(
                            out=AB[:, 1024:2048], in0=RPB[:], in1=SIN2[:], op=OP.mult))
                        r.do("vector", lambda E, S: E.drain())
                        r.do("vector", lambda E, S: E.tensor_tensor(
                            out=R1S[:], in0=AB[:, 0:1024], in1=AB[:, 1024:2048],
                            op=OP.subtract))
                        r.do("vector", lambda E, S: E.drain())
                        r.do("vector", lambda E, S: E.tensor_tensor(
                            out=AB[:, 0:1024], in0=RPA[:], in1=SIN2[:], op=OP.mult))
                        rope_free = r.do("vector", lambda E, S: E.tensor_tensor(
                            out=AB[:, 1024:2048], in0=RPB[:], in1=COS2[:], op=OP.mult),
                            inc="sV")
                        r.do("vector", lambda E, S: E.drain())
                        v_r2 = r.do("vector", lambda E, S: E.tensor_tensor(
                            out=R2S[:], in0=AB[:, 0:1024], in1=AB[:, 1024:2048],
                            op=OP.add), inc="sV")
                        r.wait("sync", "sV", v_r2)
                        dh = dst_dram.rearrange("(h d) t -> h d t", d=128)
                        h0 = dst_head0 + 2 * p
                        dma("sync", "sD", dh[h0, 0:64, :], R1S[0:64, :])
                        dma("sync", "sD", dh[h0 + 1, 0:64, :], R1S[64:128, :])
                        dma("sync", "sD", dh[h0, 64:128, :], R2S[0:64, :])
                        detang[p] = dma("sync", "sD", dh[h0 + 1, 64:128, :],
                                        R2S[64:128, :])
                return max(detang.values())

            qdone = rope_pairs(wQb, 32, QB_d, 0)
            agk = agin.rearrange("(r t) c -> r t c", r=2)[0] if False else None
            kdone = rope_pairs(wKb, 8, agin[0:1024, :], 0)
            r.wait("gpsimd", "sD", kdone)

            def ag_dst(tch, oh, src):
                return (agin[1024 + tch * 128:1024 + (tch + 1) * 128,
                             oh * 512:(oh + 1) * 512], src)
            vE, vF = tm_proj(wVb, ("sV", ln), ag_dst, bias_bf_row=None)

            r.wait("gpsimd", "sD", max(qdone, kdone))
            r.wait("gpsimd", "sE", vE)
            r.wait("gpsimd", "sF", vF)
            cc = r.do("gpsimd", lambda E, S: E.collective_compute(
                "AllGather", OP.bypass,
                replica_groups=[[0, 1], [2, 3], [4, 5], [6, 7]],
                ins=[agin_t.ap().opt()], outs=[agout_t.ap().opt()]), inc="sC")

            r.wait("sync", "sV", r.n["sV"])
            r.wait("sync", "sT", r.n["sT"])
            mload = dma("sync", "sD2", SCR[:, 0:8192].rearrange("p (s f) -> p s f", f=512),
                        cmask_d.rearrange("(s p) f -> p s f", p=128))
            KG = SCR[:, 8192:10240]
            VG = SCR[:, 10240:12288]
            ENT = [SCR[:, 12288 + i * 512:12288 + (i + 1) * 512] for i in range(3)]
            QHs = [SCR[:, 13824:14848], SCR[:, 14848:15872]]
            qh_user = {}
            r.wait("sync", "sC", cc)
            ao_last = None
            kv_consumed = None
            for g in range(8):
                if kv_consumed is not None:
                    r.wait("sync", "sT", kv_consumed)
                dma("sync", "sD", KG[:, 0:512],
                    agout[g * 128:(g + 1) * 128, 0:512])
                dma("sync", "sD", KG[:, 1536:2048],
                    agout[g * 128:(g + 1) * 128, 512:1024])
                dma("sync", "sD", KG[:, 512:1024],
                    agout[2048 + g * 128:2048 + (g + 1) * 128, 0:512])
                dma("sync", "sD", KG[:, 1024:1536],
                    agout[2048 + g * 128:2048 + (g + 1) * 128, 512:1024])
                dma("sync", "sD", VG.rearrange("p (c d) -> p c d", d=128)[:, 0:4, :],
                    agout.rearrange("(rr t) c -> rr t c", rr=4)[1]
                    .rearrange("(c p) d2 -> p c d2", p=128)[:, :, g * 128:(g + 1) * 128]
                    if False else
                    agout[1024:1536, g * 128:(g + 1) * 128]
                    .rearrange("(c p) d -> p c d", p=128))
                dma("sync", "sD", VG.rearrange("p (c d) -> p c d", d=128)[:, 4:12, :],
                    agout[3072:4096, g * 128:(g + 1) * 128]
                    .rearrange("(c p) d -> p c d", p=128))
                kvd = dma("sync", "sD", VG.rearrange("p (c d) -> p c d", d=128)[:, 12:16, :],
                          agout[1536:2048, g * 128:(g + 1) * 128]
                          .rearrange("(c p) d -> p c d", p=128))
                for j4 in range(4):
                    h = 4 * g + j4
                    hs = h % 2
                    hsem = "sE" if hs == 0 else "sF"
                    QH = QHs[hs]
                    if hs in qh_user:
                        r.wait("sync", "sT", qh_user[hs])
                    qh = dma("sync", hsem, QH, QB_d[h * 128:(h + 1) * 128, :])
                    r.wait("tensor", "sD", kvd)
                    r.wait("tensor", hsem, qh)
                    r.wait("tensor", "sD2", mload)
                    for s in range(2):
                        nk = 8 if s == 0 else 16
                        ent_free = [None, None, None]
                        bank_wait(4)
                        bank_wait(6)
                        pend = None

                        def ab_tail(c, en_sem, eb):
                            st, sp = (c == 0), (c == nk - 1)
                            r.wait("tensor", en_sem[0], en_sem[1])
                            r.do("tensor", lambda E, S, eb=eb, st=st, sp=sp: E.matmul(
                                p4[:], ONES[:, 0:128], eb, start=st, stop=sp))
                            ent_free[c % 3] = r.do(
                                "tensor", lambda E, S, eb=eb, st=st, sp=sp: E.matmul(
                                    p6[:], VG[:, c * 128:(c + 1) * 128], eb,
                                    start=st, stop=sp), inc="sT")

                        for c in range(nk):
                            sbank = P[c % 2]
                            bank_wait(c % 2)
                            mm = r.do("tensor",
                                      lambda E, S, c=c, s=s, sbank=sbank, qhb=QH: E.matmul(
                                          sbank[:], KG[:, c * 128:(c + 1) * 128],
                                          qhb[:, s * TH:(s + 1) * TH], start=True, stop=True),
                                      inc="sT")
                            eb = ENT[c % 3]
                            if ent_free[c % 3] is not None:
                                r.wait("scalar", "sT", ent_free[c % 3])
                            r.wait("scalar", "sT", mm)
                            s_e = r.do("scalar", lambda E, S, eb=eb, sbank=sbank:
                                       E.activation(out=eb, in_=sbank[:], func=AF.Exp,
                                                    scale=SCALE, bias=np.float32(BSHIFT)),
                                       inc="sS")
                            bank_free[c % 2] = ("sS", s_e)
                            masked = (s == 0) or (c >= 8)
                            if masked:
                                slot = c if s == 0 else (8 + c - 8)
                                mk = SCR[:, slot * 512:(slot + 1) * 512]
                                r.wait("vector", "sS", s_e)
                                env = r.do("vector", lambda E, S, eb=eb, mk=mk:
                                           E.tensor_tensor(out=eb, in0=mk, in1=eb,
                                                           op=OP.mult), inc="sV")
                                en_sem = ("sV", env)
                            else:
                                en_sem = ("sS", s_e)
                            if pend is not None:
                                ab_tail(*pend)
                            pend = (c, en_sem, eb)
                        ab_tail(*pend)
                        t_pv = ent_free[(nk - 1) % 3]
                        r.wait("scalar", "sT", t_pv)
                        s_l = r.do("scalar", lambda E, S: E.activation(
                            out=ST[0:1, 0:512], in_=p4[0:1, :], func=AF.Ln), inc="sS")
                        bank_free[4] = ("sS", s_l)
                        bank_wait(7)
                        r.wait("tensor", "sS", s_l)
                        t_b = r.do("tensor", lambda E, S: E.matmul(
                            p7[:], ONESF[:], ST[0:1, 0:512], start=True, stop=True), inc="sT")
                        r.wait("scalar", "sT", t_b)
                        s_r = r.do("scalar", lambda E, S, s=s: E.activation(
                            out=RB[:, s * 512:(s + 1) * 512], in_=p7[:], func=AF.Exp,
                            scale=-1.0), inc="sS")
                        bank_free[7] = ("sS", s_r)
                        r.wait("vector", "sS", s_r)
                        c0 = h * T + s * TH
                        ao_last = r.do("vector", lambda E, S, c0=c0, s=s: E.tensor_tensor(
                            out=B2[:, c0:c0 + TH], in0=p6[:], in1=RB[:, s * 512:(s + 1) * 512],
                            op=OP.mult), inc="sV")
                        bank_free[6] = ("sV", ao_last)
                    qh_user[hs] = r.n["sT"]
                kv_consumed = r.n["sT"]
            fm_proj(woB, NC, ("sV", ao_last), sink_resid())
            return r.n["sV"]

        v = sub1()
        if debug_tap == 1:
            r.wait("gpsimd", "sV", v)
            dma("gpsimd", "sW", dbg.rearrange("(c d) t -> d c t", d=128),
                X[:].rearrange("d (c t) -> d c t", t=T))
        if n_sub >= 2:
            v = sub_ffn(v)
            if debug_tap == 2:
                r.wait("gpsimd", "sV", v)
                dma("gpsimd", "sW", dbg.rearrange("(c d) t -> d c t", d=128),
                    X[:].rearrange("d (c t) -> d c t", t=T))
        if n_sub >= 3:
            v = sub3(v)
            if debug_tap == 3:
                r.wait("gpsimd", "sV", v)
                dma("gpsimd", "sW", dbg.rearrange("(c d) t -> d c t", d=128),
                    X[:].rearrange("d (c t) -> d c t", t=T))
        if n_sub >= 4:
            v = sub_ffn(v)

        r.wait("gpsimd", "sV", v)
        fin = dma("gpsimd", "sW", out.rearrange("(c d) t -> d c t", d=128),
                  X[:].rearrange("d (c t) -> d c t", t=T))
        r.wait("gpsimd", "sW", fin)

        @block.sync
        def _(E):
            for f in r.ops["sync"]:
                f(E, S)

        @block.gpsimd
        def _(E):
            for f in r.ops["gpsimd"]:
                f(E, S)

        @block.scalar
        def _(E):
            for f in r.ops["scalar"]:
                f(E, S)

        @block.vector
        def _(E):
            for f in r.ops["vector"]:
                f(E, S)

        @block.tensor
        def _(E):
            for f in r.ops["tensor"]:
                f(E, S)

    return nc



def _prep_shared(i):
    f32 = np.float32
    sh = {}
    sh["wqT"] = np.ascontiguousarray(i["L0_attn_a_Wq"].T.astype(f32))
    sh["wkT"] = np.ascontiguousarray(i["L0_attn_a_Wk"].T.astype(f32))
    sh["wvT"] = np.ascontiguousarray(i["L0_attn_a_Wv"].T.astype(f32))
    sh["woT"] = np.ascontiguousarray(i["L0_attn_a_Wo"].T.astype(f32))
    sh["bq_t"] = np.ascontiguousarray(i["L0_attn_a_bq"].reshape(32, 128).astype(f32))
    sh["bk_t"] = np.ascontiguousarray(i["L0_attn_a_bk"].reshape(32, 128).astype(f32))
    sh["bv_d"] = np.ascontiguousarray(i["L0_attn_a_bv"].astype(f32))
    sh["bo_d"] = np.ascontiguousarray(i["L0_attn_a_bo"].astype(f32))
    sh["bo_t"] = np.ascontiguousarray(i["L0_attn_a_bo"].reshape(32, 128).astype(f32))
    sh["bbase_t"] = np.ascontiguousarray(i["L0_ffn_b_base"].reshape(32, 128).astype(f32))
    wqkv = i["L0_attn_b_w_qkv"].astype(f32)
    sh["wQb"] = np.ascontiguousarray(wqkv[:, 0:4096][:, _pair_perm(32)])
    sh["wKb"] = np.ascontiguousarray(wqkv[:, 4096:5120][:, _pair_perm(8)])
    sh["wVb"] = np.ascontiguousarray(wqkv[:, 5120:6144])
    sh["woB"] = np.ascontiguousarray(i["L0_attn_b_w_o"].astype(f32))
    sh["wbaseT"] = np.ascontiguousarray(i["L0_ffn_W_base"].T.astype(f32))
    sh["bbase_d"] = np.ascontiguousarray(i["L0_ffn_b_base"].astype(f32))
    sh["wrouT"] = np.ascontiguousarray(i["L0_ffn_W_router"].T.astype(f32))
    sh["brou"] = np.ascontiguousarray(i["L0_ffn_b_router"].reshape(8, 1).astype(f32))
    sh["a_all"] = np.ascontiguousarray(
        i["L0_ffn_lora_A"].transpose(1, 0, 2).reshape(4096, 128).astype(f32))
    sh["b_all"] = np.ascontiguousarray(i["L0_ffn_lora_B"].reshape(128, 4096).astype(f32))
    sh["maskil_d"] = np.ascontiguousarray(
        ((np.arange(128)[:, None] % 4) == (np.arange(128)[None, :] % 4)).astype(BF))
    sh["id32_d"] = np.eye(128, dtype=f32)
    sh["ones_d"] = np.ones((128, 512), BF)
    sh["onesf_d"] = np.ones((1, 128), f32)
    pr = np.zeros((8, 128), np.float32)
    for e in range(8):
        pr[e, e * 16:(e + 1) * 16] = 1.0
    sh["prep_d"] = np.ascontiguousarray(pr.astype(BF))
    return sh


def kernel(**inputs):
    x = np.asarray(inputs["L0_attn_a_x"], np.float32)
    sh = _prep_shared(inputs)
    nc = build_program(n_sub=N_SUB, debug_tap=DEBUG_TAP)
    in_maps = []
    for c in range(8):
        b = c // 2
        pos = _core_positions(c)
        cos2, sin2 = _cos_sin(pos)
        m = dict(sh)
        m["x_in"] = np.ascontiguousarray(x[b][pos].T)
        m["cos_d"] = cos2
        m["sin_d"] = sin2
        m["cmask_d"] = _causal_masks(c)
        in_maps.append(m)
    res = run_bass_kernel_spmd(nc, in_maps, core_ids=list(range(8)))
    outp = np.empty((4, 2048, 4096), np.float32)
    for c in range(8):
        b = c // 2
        pos = _core_positions(c)
        outp[b][pos] = res.results[c]["out"].T
    if DEBUG_TAP is not None:
        kernel._dbg = [res.results[c]["dbg"].T for c in range(8)]
    return outp
